# version 01 (final)
# baseline (speedup 1.0000x reference)
import contextlib
import math

import numpy as np
import ml_dtypes
import concourse.bass as bass
import concourse.mybir as mybir
from concourse.bass_utils import run_bass_kernel_spmd

F32 = mybir.dt.float32
BF16 = mybir.dt.bfloat16
AF = mybir.ActivationFunctionType
ALU = mybir.AluOpType

NCORES = 8
BPC = 4
S = 2048
D = 1024
NT = S // 128
KD = D // 128
EPS_LN = 1e-5



class Ctx:

    def __init__(self, nc, es):
        self.nc = nc
        self.es = es
        self.block = None
        self._sems = {}
        self.cnt = {}
        self.buf = {}

    def sem(self, name):
        if name not in self._sems:
            self._sems[name] = self.es.enter_context(self.nc.semaphore(name))
            self.cnt[name] = 0
        return self._sems[name]

    def inc(self, name, k=1):
        self.sem(name)
        self.cnt[name] += k
        return self.cnt[name]

    def val(self, name):
        self.sem(name)
        return self.cnt[name]

    def bar(self, engname, eng, inst):
        nm = "bar_" + engname
        self.sem(nm)
        self.cnt[nm] += 1
        inst.then_inc(self._sems[nm], 1)
        eng.wait_ge(self._sems[nm], self.cnt[nm])

    def constcol(self, value):
        if "cconst" not in self.buf:
            self.buf["cconst"] = self.sbuf("cconst", [128, 16], F32)
            self.cconst_vals = []
        if value not in self.cconst_vals:
            self.cconst_vals.append(value)
        i = self.cconst_vals.index(value)
        return self.buf["cconst"][:, i:i + 1]

    def sbuf(self, name, shape, dtype):
        t = self.es.enter_context(self.nc.sbuf_tensor(name, list(shape), dtype))
        self.buf[name] = t
        return t

    def psum(self, name, shape, dtype):
        t = self.es.enter_context(self.nc.psum_tensor(name, list(shape), dtype))
        self.buf[name] = t
        return t


class Rot:

    def __init__(self, C, name, slots):
        self.C = C
        self.name = name
        self.slots = slots
        self.n = len(slots)
        self.done = name + "_d"
        C.sem(self.done)
        self.p_i = 0
        self.c_i = 0
        self.p_assign = []
        self.c_assign = []

    def _free(self, tag):
        return self.name + "_f" + tag

    def p_slot(self):
        return self.slots[self.p_i % self.n]

    def p_wait(self, eng):
        if self.p_i >= self.n:
            upto = self.p_i - self.n
            need = {}
            for tag in self.p_assign[:upto + 1]:
                need[tag] = need.get(tag, 0) + 1
            for tag, k in need.items():
                eng.wait_ge(self.C.sem(self._free(tag)), k)

    def p_done(self, inst, ctag):
        self.p_i += 1
        self.p_assign.append(ctag)
        self.C.sem(self._free(ctag))
        self.C.cnt[self.done] += 1
        inst.then_inc(self.C._sems[self.done], 1)

    def self_bar_p(self, eng):
        eng.wait_ge(self.C.sem(self.done), self.C.cnt[self.done])

    def c_slot(self):
        return self.slots[self.c_i % self.n]

    def c_wait(self, eng):
        eng.wait_ge(self.C.sem(self.done), self.c_i + 1)

    def c_done(self, inst, tag="v"):
        assert self.p_assign[self.c_i] == tag, (
            f"{self.name}: consume {self.c_i} tagged {tag} but produce declared "
            f"{self.p_assign[self.c_i]}")
        self.c_i += 1
        self.c_assign.append(tag)
        nm = self._free(tag)
        self.C.cnt[nm] += 1
        inst.then_inc(self.C._sems[nm], 1)

    def free_count_wait(self, eng):
        need = {}
        for tag in self.p_assign[:self.c_i]:
            need[tag] = need.get(tag, 0) + 1
        for tag, k in need.items():
            eng.wait_ge(self.C.sem(self._free(tag)), k)



def emit_consts(C):
    nc = C.nc
    ident = C.sbuf("ident", [128, 128], BF16)
    ident16 = C.sbuf("ident16", [128, 128], F16)
    ones_row = C.sbuf("ones_row", [1, 512], BF16)
    if "onesf_row" not in C.buf:
        C.sbuf("onesf_row", [128, 128], F32)
    onesf_row = C.buf["onesf_row"]
    ones_col = C.sbuf("ones_col", [128, 1], BF16)

    @C.block.gpsimd
    def _(g):
        gp = C.sem("gpw")
        for i, v in enumerate(getattr(C, "cconst_vals", [])):
            g.memset(C.buf["cconst"][:, i:i + 1], v).then_inc(gp, 1)
            C.cnt["gpw"] += 1
        g.memset(ident[:], 1.0).then_inc(gp, 1)
        g.memset(ident16[:], 1.0).then_inc(gp, 1)
        C.cnt["gpw"] += 2
        g.wait_ge(gp, C.cnt["gpw"])
        g.affine_select(out=ident[:], in_=ident[:], pattern=[[1, 128]],
                        compare_op=ALU.is_equal, fill=0.0,
                        base=0, channel_multiplier=-1).then_inc(gp, 1)
        g.affine_select(out=ident16[:], in_=ident16[:], pattern=[[1, 128]],
                        compare_op=ALU.is_equal, fill=0.0,
                        base=0, channel_multiplier=-1).then_inc(gp, 1)
        C.cnt["gpw"] += 2
        g.memset(ones_row[:], 1.0).then_inc(gp, 1)
        g.memset(onesf_row[:], 1.0).then_inc(gp, 1)
        g.memset(ones_col[:], 1.0).then_inc(gp, 1)
        C.cnt["gpw"] += 3
        g.wait_ge(gp, C.cnt["gpw"])
        g.nop().then_inc(C.sem("consts"), 1)
    C.cnt["consts"] += 1


def wait_consts(C, eng):
    eng.wait_ge(C.sem("consts"), C.cnt["consts"])


def emit_load_x(C, x_hbm, b):
    res = C.buf["res"]

    @C.block.sync
    def _(sy):
        if b > 0:
            sy.wait_ge(C.sem("res_store"), C.cnt["res_store"])
        src = x_hbm[b].rearrange("(t p) d -> p t d", p=128)
        dst = res[:].rearrange("p (t d) -> p t d", d=D)
        sy.dma_start(out=dst, in_=src).then_inc(C.sem("res_load"), 16)
    C.cnt["res_load"] += 16


def emit_store_x(C, y_hbm, b):
    res = C.buf["res"]

    @C.block.sync
    def _(sy):
        sy.wait_ge(C.sem("res_load"), C.cnt["res_load"])
        sy.wait_ge(C.sem("res_upd"), C.cnt["res_upd"])
        dst = y_hbm[b].rearrange("(t p) d -> p t d", p=128)
        src = res[:].rearrange("p (t d) -> p t d", d=D)
        sy.dma_start(out=dst, in_=src).then_inc(C.sem("res_store"), 16)
    C.cnt["res_store"] += 16


def emit_ln_stats(C, src_name="res", wait_sems=()):
    res = C.buf[src_name]
    bn6, mv = C.buf["bn6"], C.buf["mv16"]
    nm, rs = C.buf["nm16"], C.buf["rs16"]

    @C.block.vector
    def _(v):
        v.wait_ge(C.sem("res_load"), C.cnt["res_load"])
        v.wait_ge(C.sem("res_upd"), C.cnt["res_upd"])
        for (s, c) in wait_sems:
            v.wait_ge(C.sem(s), c)
        last = None
        for i in range(NT):
            t = res[:, i * D:(i + 1) * D]
            for j in range(4):
                last = v.bn_stats(out=bn6[:, (4 * i + j) * 6:(4 * i + j + 1) * 6],
                                  in_=t[:, j * 256:(j + 1) * 256])
        last.then_inc(C.sem("ln_g1"), 1)
        C.cnt["ln_g1"] += 1
        v.wait_ge(C.sem("ln_g1"), C.cnt["ln_g1"])
        for i in range(NT):
            last = v.bn_aggr(out=mv[:, 2 * i:2 * i + 2], in_=bn6[:, 24 * i:24 * i + 24])
        last.then_inc(C.sem("ln_g2"), 1)
        C.cnt["ln_g2"] += 1
        v.wait_ge(C.sem("ln_g2"), C.cnt["ln_g2"])
        v.tensor_scalar(out=nm[:], in0=mv[:].rearrange("p (t two) -> p two t", two=2)[:, 0, :],
                        scalar1=-1.0, scalar2=None, op0=ALU.mult,
                        op1=ALU.bypass).then_inc(C.sem("ln_stats"), 1)
    C.cnt["ln_stats"] += 1

    @C.block.scalar
    def _(s):
        s.wait_ge(C.sem("ln_stats"), C.cnt["ln_stats"])
        wait_consts(C, s)
        var_view = mv[:].rearrange("p (t two) -> p two t", two=2)[:, 1, :]
        inst = s.activation(out=rs[:], in_=var_view, func=AF.Ln,
                            bias=C.constcol(EPS_LN), scale=1.0)
        C.bar("s", s, inst)
        s.activation(out=rs[:], in_=rs[:], func=AF.Exp,
                     scale=-0.5).then_inc(C.sem("ln_rs"), 1)
    C.cnt["ln_rs"] += 1

    @C.block.vector
    def _(v):
        v.wait_ge(C.sem("ln_rs"), C.cnt["ln_rs"])
        v.wait_ge(C.sem("ln_stats"), C.cnt["ln_stats"])
        v.tensor_tensor(out=nm[:], in0=nm[:], in1=rs[:],
                        op=ALU.mult).then_inc(C.sem("ln_nmrs"), 1)
    C.cnt["ln_nmrs"] += 1


def emit_ln_t2fm(C, src_name="res", dst_name="xnT", wait_sems=(), zero_col=False):
    emit_ln_stats(C, src_name, wait_sems)
    res = C.buf[src_name]
    nm, rs = C.buf["nm16"], C.buf["rs16"]
    dst = C.buf[dst_name]
    rXN = C.buf["rXN"]
    rotT = C.buf["rotT"]
    row = S + 1 if zero_col else S
    off0 = 1 if zero_col else 0

    for g in range(8):
        @C.block.scalar
        def _(s, g=g):
            if g == 0:
                s.wait_ge(C.sem("ln_nmrs"), C.cnt["ln_nmrs"])
                s.wait_ge(C.sem("ln_rs"), C.cnt["ln_rs"])
            rXN.p_wait(s)
            slot = rXN.p_slot()
            last = None
            for j in range(2):
                i = 2 * g + j
                last = s.activation(out=slot[:, j * D:(j + 1) * D],
                                    in_=res[:, i * D:(i + 1) * D],
                                    func=AF.Identity, bias=nm[:, i:i + 1], scale=rs[:, i:i + 1])
            rXN.p_done(last, "t")

        @C.block.tensor
        def _(t, g=g):
            rXN.c_wait(t)
            wait_consts(C, t)
            slot = rXN.c_slot()
            for kg in range(2):
                rotT.p_wait(t)
                ps = rotT.p_slot()
                last = None
                for i in range(4):
                    kd = kg * 4 + i
                    for j in range(2):
                        last = t.transpose(ps[:, i * 256 + j * 128: i * 256 + (j + 1) * 128],
                                           slot[:, j * D + kd * 128: j * D + kd * 128 + 128],
                                           C.buf["ident"][:])
                rotT.p_done(last, "v")
            rotT.self_bar_p(t)
            rXN.c_done(t.nop(), "t")

        @C.block.vector
        def _(v, g=g):
            for kg in range(2):
                rotT.c_wait(v)
                ps = rotT.c_slot()
                for i in range(4):
                    kd = kg * 4 + i
                    inst = v.tensor_copy(out=dst[:, kd * row + off0 + g * 256:
                                                 kd * row + off0 + (g + 1) * 256],
                                         in_=ps[:, i * 256:(i + 1) * 256])
                if g == 7 and kg == 1:
                    C.bar("v", v, inst)
                    rotT.c_done(v.nop())
                else:
                    rotT.c_done(inst)

    @C.block.vector
    def _(v):
        v.nop().then_inc(C.sem(dst_name + "_done"), 1)
    C.cnt[dst_name + "_done"] += 1



RMAGIC = 12582912.0
TWO_PI = 2.0 * math.pi


def alloc_l0a_consts(C):
    C.sbuf("iotaT", [128, S], F32)
    C.sbuf("ropeC", [128, S], BF16)
    C.sbuf("ropeS", [128, S], BF16)
    C.sbuf("ifcol", [128, 1], F32)
    C.sbuf("lgq", [128, 8], F32)
    C.sbuf("lgk", [128, 8], F32)
    C.sbuf("tri", [128, 896], BF16)


def emit_l0a_consts(C):
    nc = C.nc
    full = "tri" in C.buf
    iotaT, ropeC, ropeS = C.buf["iotaT"], C.buf["ropeC"], C.buf["ropeS"]
    ifcol = C.buf["ifcol"]
    lgq = C.buf["lgq"] if full else None
    lgk = C.buf["lgk"] if full else None
    tri = C.buf["tri"] if full else None
    og = C.buf["og"]
    tmpA = og[:, 0:4096].bitcast(F32)
    tmpB = og[:, 4096:8192].bitcast(F32)

    @C.block.gpsimd
    def _(g):
        gp = C.sem("gpw")
        g.iota(out=iotaT[:], pattern=[[1, S]], base=0, channel_multiplier=0,
               allow_small_or_imprecise_dtypes=True).then_inc(gp, 1)
        C.cnt["gpw"] += 1
        for qrt in range(4):
            g.iota(out=ifcol[qrt * 32:(qrt + 1) * 32, :], pattern=[[0, 1]], base=0,
                   channel_multiplier=1,
                   allow_small_or_imprecise_dtypes=True).then_inc(gp, 1)
            C.cnt["gpw"] += 1
        if full:
            for hp in range(8):
                for hh in range(2):
                    lg = math.log(1.0 - 2.0 ** (-5.0 - (2 * hp + hh)))
                    g.memset(lgq[hh * 64:(hh + 1) * 64, hp:hp + 1], lg).then_inc(gp, 1)
                    g.memset(lgk[hh * 64:(hh + 1) * 64, hp:hp + 1], -lg).then_inc(gp, 1)
                    C.cnt["gpw"] += 2
            g.memset(tri[:], 1.0).then_inc(gp, 1)
            C.cnt["gpw"] += 1
            g.wait_ge(gp, C.cnt["gpw"])
            g.affine_select(out=tri[:], in_=tri[:],
                            pattern=[[1, 896]], compare_op=ALU.is_ge, fill=0.0,
                            base=-384, channel_multiplier=-1).then_inc(gp, 1)
            C.cnt["gpw"] += 1
        g.wait_ge(gp, C.cnt["gpw"])
        g.nop().then_inc(C.sem("l0c_g"), 1)
    C.cnt["l0c_g"] += 1

    @C.block.scalar
    def _(s):
        s.wait_ge(C.sem("l0c_g"), C.cnt["l0c_g"])
        s.activation(out=ifcol[:], in_=ifcol[:], func=AF.Exp,
                     scale=-math.log(10000.0) / 32.0).then_inc(C.sem("l0c_if"), 1)
    C.cnt["l0c_if"] += 1

    @C.block.vector
    def _(v):
        v.wait_ge(C.sem("l0c_if"), C.cnt["l0c_if"])
        inst = v.tensor_scalar(out=tmpA[:], in0=iotaT[:], scalar1=ifcol[:], scalar2=None,
                               op0=ALU.mult, op1=ALU.bypass)
        C.bar("v", v, inst)
        inst = v.tensor_scalar(out=tmpB[:], in0=tmpA[:], scalar1=1.0 / TWO_PI, scalar2=RMAGIC,
                               op0=ALU.mult, op1=ALU.add)
        C.bar("v", v, inst)
        inst = v.tensor_scalar(out=tmpB[:], in0=tmpB[:], scalar1=-RMAGIC, scalar2=None,
                               op0=ALU.add, op1=ALU.bypass)
        C.bar("v", v, inst)
        inst = v.scalar_tensor_tensor(out=tmpB[:], in0=tmpB[:], scalar=-TWO_PI, in1=tmpA[:],
                                      op0=ALU.mult, op1=ALU.add)
        C.bar("v", v, inst)
        v.tensor_scalar(out=tmpB[:], in0=tmpB[:], scalar1=-math.pi, scalar2=math.pi,
                        op0=ALU.max, op1=ALU.min).then_inc(C.sem("l0c_v1"), 1)
    C.cnt["l0c_v1"] += 1

    @C.block.scalar
    def _(s):
        s.wait_ge(C.sem("l0c_v1"), C.cnt["l0c_v1"])
        s.activation(out=ropeS[:], in_=tmpB[:], func=AF.Sin).then_inc(C.sem("l0c_s1"), 1)
    C.cnt["l0c_s1"] += 1

    @C.block.vector
    def _(v):
        v.wait_ge(C.sem("l0c_s1"), C.cnt["l0c_s1"])
        inst = v.tensor_scalar(out=tmpA[:], in0=tmpA[:], scalar1=math.pi / 2, scalar2=None,
                               op0=ALU.add, op1=ALU.bypass)
        C.bar("v", v, inst)
        inst = v.tensor_scalar(out=tmpB[:], in0=tmpA[:], scalar1=1.0 / TWO_PI, scalar2=RMAGIC,
                               op0=ALU.mult, op1=ALU.add)
        C.bar("v", v, inst)
        inst = v.tensor_scalar(out=tmpB[:], in0=tmpB[:], scalar1=-RMAGIC, scalar2=None,
                               op0=ALU.add, op1=ALU.bypass)
        C.bar("v", v, inst)
        inst = v.scalar_tensor_tensor(out=tmpB[:], in0=tmpB[:], scalar=-TWO_PI, in1=tmpA[:],
                                      op0=ALU.mult, op1=ALU.add)
        C.bar("v", v, inst)
        v.tensor_scalar(out=tmpB[:], in0=tmpB[:], scalar1=-math.pi, scalar2=math.pi,
                        op0=ALU.max, op1=ALU.min).then_inc(C.sem("l0c_v2"), 1)
    C.cnt["l0c_v2"] += 1

    @C.block.scalar
    def _(s):
        s.wait_ge(C.sem("l0c_v2"), C.cnt["l0c_v2"])
        s.activation(out=ropeC[:], in_=tmpB[:], func=AF.Sin).then_inc(C.sem("l0c_s2"), 1)
    C.cnt["l0c_s2"] += 1

    @C.block.vector
    def _(v):
        v.wait_ge(C.sem("l0c_s2"), C.cnt["l0c_s2"])
        for blk in range(2):
            inst = v.tensor_scalar(out=ropeS[blk * 64:blk * 64 + 32, :],
                                   in0=ropeS[blk * 64:blk * 64 + 32, :],
                                   scalar1=-1.0, scalar2=None, op0=ALU.mult, op1=ALU.bypass)
        C.bar("v", v, inst)
        v.nop().then_inc(C.sem("l0c_done"), 1)
    C.cnt["l0c_done"] += 1


def _noop_res_upd(C):
    @C.block.vector
    def _(v):
        v.nop().then_inc(C.sem("res_upd"), 1)
    C.cnt["res_upd"] += 1


def emit_l0_attn(C, W):
    nc = C.nc
    xnT = C.buf["xnT"]
    iotaT, ropeC, ropeS = C.buf["iotaT"], C.buf["ropeC"], C.buf["ropeS"]
    lgq, lgk, tri = C.buf["lgq"], C.buf["lgk"], C.buf["tri"]
    wT = C.buf["wT"]
    woT = C.buf["woT"]
    qt = C.buf["l0_q"]
    kt_ = C.buf["l0_k"]
    vtm = C.buf["l0_v"]
    gg = C.buf["l0_g"]
    bufA = C.buf["l0_bA"]
    bufSW = C.buf["l0_bSW"]
    gamq = C.buf["l0_gamq"]
    gamk = C.buf["l0_gamk"]
    og = C.buf["og"]
    ret = C.buf["retT"]
    res = C.buf["res"]
    r0 = C.buf["rot0"]
    r1 = C.buf["rot1"]
    rotT = C.buf["rotT"]

    tnames = ["l0aux_wqT", "l0aux_wkT", "l0aux_wvT", "l0aux_wgT"]

    @C.block.gpsimd
    def _(g):
        base = C.cnt["l0_wfree"]
        g.wait_ge(C.sem(r1.done), C.cnt[r1.done])
        g.dma_start(out=woT[:].rearrange("p (kt n) -> p kt n", n=1024),
                    in_=W["l0aux_woT"].rearrange("(kt p) n -> p kt n", p=128)
                    ).then_inc(C.sem("l0_wld"), 16)
        for hp in range(L0A_PAIRS):
            if base + hp > 0:
                g.wait_ge(C.sem("l0_wfree"), base + hp)
            for mi, nm_ in enumerate(tnames):
                src = W[nm_].rearrange("(kt p) n -> p kt n", p=128)[:, :, 128 * hp:128 * (hp + 1)]
                dst = wT[:, mi * 1024:(mi + 1) * 1024].rearrange("p (kt n) -> p kt n", n=128)
                g.dma_start(out=dst, in_=src).then_inc(C.sem("l0_wld"), 16)
    C.cnt["l0_wld"] += 16 * (1 + 4 * L0A_PAIRS)
    ld_base = C.val("l0_wld") - 16 * (1 + 4 * L0A_PAIRS)

    for hp in range(L0A_PAIRS):
        @C.block.scalar
        def _(s, hp=hp):
            s.wait_ge(C.sem("l0c_done"), C.cnt["l0c_done"])
            if hp == 0:
                s.wait_ge(C.sem("xnT_done"), C.cnt["xnT_done"])
            else:
                s.wait_ge(C.sem("l0_qkdone"), C.cnt["l0_qkdone"])
            s.activation(out=gamq[:], in_=iotaT[:], func=AF.Exp,
                         scale=lgq[:, hp:hp + 1], bias=C.constcol(math.log(0.125)))
            s.activation(out=gamk[:], in_=iotaT[:], func=AF.Exp,
                         scale=lgk[:, hp:hp + 1]).then_inc(C.sem("l0_gam"), 1)
        C.cnt["l0_gam"] += 1

        for mi in range(4):
            @C.block.tensor
            def _(t, hp=hp, mi=mi):
                if mi == 0:
                    t.wait_ge(C.sem("l0_wld"), ld_base + 16 * (1 + 4 * (hp + 1)))
                    if hp == 0:
                        t.wait_ge(C.sem("xnT_done"), C.cnt["xnT_done"])
                for tc in range(4):
                    r0.p_wait(t)
                    slot = r0.p_slot()
                    for kt2 in range(KD):
                        inst = t.matmul(slot,
                                        wT[:, mi * 1024 + kt2 * 128: mi * 1024 + (kt2 + 1) * 128],
                                        xnT[:, kt2 * S + tc * 512: kt2 * S + tc * 512 + 512],
                                        start=(kt2 == 0), stop=(kt2 == KD - 1))
                    r0.p_done(inst, "s")
                if mi == 3:
                    r0.self_bar_p(t)
                    t.nop().then_inc(C.sem("l0_wfree"), 1)
            if mi == 3:
                C.cnt["l0_wfree"] += 1

            if mi in (0, 1):
                @C.block.scalar
                def _(s, hp=hp, mi=mi):
                    if mi == 1:
                        s.wait_ge(C.sem("l0_qk"), C.cnt["l0_qk"])
                    for tc in range(4):
                        r0.c_wait(s)
                        inst = s.copy(out=bufA[:, tc * 512:(tc + 1) * 512], in_=r0.c_slot())
                        if tc < 3:
                            r0.c_done(inst, "s")
                    C.bar("s", s, inst)
                    r0.c_done(s.nop(), "s")
                    if L0A_SNAP and hp == 0 and mi == 0 and C.cnt["l0_qkraw"] == 2 * L0A_PAIRS:
                        s.copy(out=C.buf["woT"][:, 0:4096], in_=wT[:])
                        s.copy(out=C.buf["woT"][:, 4096:6144], in_=bufA[:])
                        s.copy(out=C.buf["woT"][:, 6144:8192], in_=xnT[:, 0:2048])
                    s.nop().then_inc(C.sem("l0_qkraw"), 1)
                C.cnt["l0_qkraw"] += 1

                @C.block.vector
                def _(v, hp=hp, mi=mi):
                    v.wait_ge(C.sem("l0_qkraw"), C.cnt["l0_qkraw"])
                    if mi == 0:
                        v.wait_ge(C.sem("l0_gam"), C.cnt["l0_gam"])
                    for blk in range(2):
                        b0 = blk * 64
                        v.tensor_copy(out=bufSW[b0:b0 + 32, :], in_=bufA[b0 + 32:b0 + 64, :])
                        inst = v.tensor_copy(out=bufSW[b0 + 32:b0 + 64, :],
                                             in_=bufA[b0:b0 + 32, :])
                    C.bar("v", v, inst)
                    inst = v.tensor_tensor(out=bufA[:], in0=bufA[:], in1=ropeC[:], op=ALU.mult)
                    inst2 = v.tensor_tensor(out=bufSW[:], in0=bufSW[:], in1=ropeS[:], op=ALU.mult)
                    C.bar("v", v, inst2)
                    inst = v.tensor_tensor(out=bufA[:], in0=bufA[:], in1=bufSW[:], op=ALU.add)
                    C.bar("v", v, inst)
                    dst = qt if mi == 0 else kt_
                    gam = gamq if mi == 0 else gamk
                    v.tensor_tensor(out=dst[:], in0=bufA[:], in1=gam[:],
                                    op=ALU.mult).then_inc(C.sem("l0_qk"), 1)
                C.cnt["l0_qk"] += 1
            elif mi == 2:
                @C.block.scalar
                def _(s, hp=hp):
                    s.wait_ge(C.sem("l0_qk"), C.cnt["l0_qk"])
                    for tc in range(4):
                        r0.c_wait(s)
                        inst = s.copy(out=bufA[:, tc * 512:(tc + 1) * 512], in_=r0.c_slot())
                        if tc < 3:
                            r0.c_done(inst, "s")
                    C.bar("s", s, inst)
                    r0.c_done(s.nop(), "s")
                    s.nop().then_inc(C.sem("l0_vfm"), 1)
                C.cnt["l0_vfm"] += 1

                @C.block.tensor
                def _(t, hp=hp):
                    t.wait_ge(C.sem("l0_vfm"), C.cnt["l0_vfm"])
                    for g2 in range(2):
                        rotT.p_wait(t)
                        slot = rotT.p_slot()
                        last = None
                        for j in range(8):
                            tt = g2 * 8 + j
                            last = t.transpose(slot[:, j * 128:(j + 1) * 128],
                                               bufA[:, tt * 128:(tt + 1) * 128], C.buf["ident"][:])
                        rotT.p_done(last, "v")

                @C.block.vector
                def _(v, hp=hp):
                    for g2 in range(2):
                        rotT.c_wait(v)
                        inst = v.tensor_copy(out=vtm[:, g2 * 1024:(g2 + 1) * 1024],
                                             in_=rotT.c_slot()[:, 0:1024])
                        if g2 < 1:
                            rotT.c_done(inst)
                    C.bar("v", v, inst)
                    rotT.c_done(v.nop())
                    v.nop().then_inc(C.sem("l0_vtm"), 1)
                C.cnt["l0_vtm"] += 1
            else:
                @C.block.scalar
                def _(s, hp=hp):
                    for tc in range(4):
                        r0.c_wait(s)
                        inst = s.activation(out=gg[:, tc * 512:(tc + 1) * 512], in_=r0.c_slot(),
                                            func=AF.Silu)
                        if tc < 3:
                            r0.c_done(inst, "s")
                    C.bar("s", s, inst)
                    r0.c_done(s.nop(), "s")
                    s.nop().then_inc(C.sem("l0_g"), 1)
                C.cnt["l0_g"] += 1

        for hh in range(2 if L0A_ATTN else 0):
            hoff = hh * 64
            for qb in range(4):
                nkt = 4 * qb + 4
                blkid = C.cnt["l0_rete"]

                @C.block.tensor
                def _(t, hp=hp, hh=hh, qb=qb, hoff=hoff, nkt=nkt):
                    if hh == 0 and qb == 0:
                        t.wait_ge(C.sem("l0_qk"), C.cnt["l0_qk"])
                    for kt2 in range(nkt):
                        r0.p_wait(t)
                        inst = t.matmul(r0.p_slot(),
                                        kt_[hoff:hoff + 64, kt2 * 128:(kt2 + 1) * 128],
                                        qt[hoff:hoff + 64, qb * 512:(qb + 1) * 512],
                                        start=True, stop=True)
                        r0.p_done(inst, "v")

                attv_prev = C.cnt[r1.done]

                @C.block.vector
                def _(v, qb=qb, nkt=nkt, attv_prev=attv_prev):
                    v.wait_ge(C.sem(r1.done), attv_prev)
                    inst = None
                    for kt2 in range(nkt):
                        r0.c_wait(v)
                        if kt2 >= 4 * qb:
                            rdiag = kt2 - 4 * qb
                            inst = v.tensor_tensor(out=ret[:, kt2 * 512:(kt2 + 1) * 512],
                                                   in0=r0.c_slot(),
                                                   in1=tri[:, (3 - rdiag) * 128:(3 - rdiag) * 128 + 512],
                                                   op=ALU.mult)
                        else:
                            inst = v.tensor_copy(out=ret[:, kt2 * 512:(kt2 + 1) * 512],
                                                 in_=r0.c_slot())
                        if kt2 < nkt - 1:
                            r0.c_done(inst)
                    C.bar("v", v, inst)
                    r0.c_done(v.nop())
                    v.nop().then_inc(C.sem("l0_rete"), 1)
                C.cnt["l0_rete"] += 1

                @C.block.tensor
                def _(t, hp=hp, hh=hh, qb=qb, hoff=hoff, nkt=nkt):
                    t.wait_ge(C.sem("l0_rete"), C.cnt["l0_rete"])
                    t.wait_ge(C.sem("l0_vtm"), C.cnt["l0_vtm"])
                    r1.p_wait(t)
                    slot = r1.p_slot()
                    for kt2 in range(nkt):
                        inst = t.matmul(slot[hoff:hoff + 64, 0:512],
                                        vtm[:, kt2 * 128 + hoff: kt2 * 128 + hoff + 64],
                                        ret[:, kt2 * 512:(kt2 + 1) * 512],
                                        start=(kt2 == 0), stop=(kt2 == nkt - 1),
                                        tile_position=(0, hoff))
                    r1.p_done(inst, "v")

                @C.block.vector
                def _(v, hp=hp, hh=hh, qb=qb, hoff=hoff):
                    if hh == 0 and qb == 0:
                        v.wait_ge(C.sem("l0_g"), C.cnt["l0_g"])
                    r1.c_wait(v)
                    slot = r1.c_slot()
                    od = hp % 4
                    inst = v.tensor_tensor(
                        out=og[hoff:hoff + 64, od * S + qb * 512: od * S + qb * 512 + 512],
                        in0=slot[hoff:hoff + 64, 0:512],
                        in1=gg[hoff:hoff + 64, qb * 512:(qb + 1) * 512], op=ALU.mult)
                    r1.c_done(inst)

        @C.block.vector
        def _(v, hp=hp):
            v.nop().then_inc(C.sem("l0_qkdone"), 1)
        C.cnt["l0_qkdone"] += 1

        if hp in L0A_WOPASS and L0A_WO and L0A_ATTN:
            fbase = (hp // 4) * 4
            @C.block.tensor
            def _(t, hp=hp, fbase=fbase):
                if hp == 3:
                    t.wait_ge(C.sem("l0_wld"), ld_base + 16 + 64 * 4)
                r1.free_count_wait(t)
                for tt in range(NT):
                    r1.p_wait(t)
                    slot = r1.p_slot()
                    for nc2 in range(2):
                        sl = slot[:, nc2 * 512:(nc2 + 1) * 512]
                        for kt2 in range(4):
                            kglob = fbase + kt2
                            inst = t.matmul(sl, og[:, kt2 * S + tt * 128: kt2 * S + tt * 128 + 128],
                                            woT[:, kglob * 1024 + nc2 * 512:
                                                kglob * 1024 + nc2 * 512 + 512],
                                            start=(kt2 == 0), stop=(kt2 == 3))
                    r1.p_done(inst, "v")

            @C.block.vector
            def _(v, hp=hp):
                for tt in range(NT):
                    r1.c_wait(v)
                    slot = r1.c_slot()
                    v.tensor_tensor(out=res[:, tt * D: tt * D + 512],
                                    in0=res[:, tt * D: tt * D + 512], in1=slot[:, 0:512],
                                    op=ALU.add)
                    inst = v.tensor_tensor(out=res[:, tt * D + 512: tt * D + 1024],
                                           in0=res[:, tt * D + 512: tt * D + 1024],
                                           in1=slot[:, 512:1024], op=ALU.add)
                    if tt < NT - 1:
                        r1.c_done(inst)
                C.bar("v", v, inst)
                r1.c_done(v.nop())
                v.nop().then_inc(C.sem("res_upd"), 1)
            C.cnt["res_upd"] += 1
    if not (L0A_WO and L0A_ATTN):
        _noop_res_upd(C)


def alloc_l0_attn(C):
    C.sbuf("wT", [128, 4 * 1024], BF16)
    C.sbuf("woT", [128, KD * 1024], BF16)
    C.sbuf("l0_q", [128, S], BF16)
    C.sbuf("l0_k", [128, S], BF16)
    C.sbuf("l0_v", [128, 16 * 128], BF16)
    C.sbuf("l0_g", [128, S], BF16)
    C.sbuf("l0_bA", [128, S], BF16)
    xns = C.buf["xnstage"]
    C.buf["l0_bSW"] = xns[:, 2048:4096]
    C.buf["l0_gamk"] = xns[:, 0:2048]
    C.sbuf("l0_gamq", [128, S], BF16)
    C.sbuf("og", [128, 4 * S], BF16)
    C.sbuf("retT", [128, 16 * 512], BF16)



def alloc_l1_attn(C):
    C.sbuf("l1a_wq", [128, 2 * 1024], BF16)
    C.sbuf("l1a_wk", [128, 2 * 1024], BF16)
    C.sbuf("l1a_mv", [128, 2 * 1024], BF16)
    C.sbuf("l1a_mp", [128, 4 * 1024], BF16)
    C.sbuf("l1a_qA", [128, S], BF16)
    C.sbuf("l1a_qB", [128, S], BF16)
    C.sbuf("l1a_kA", [128, S], BF16)
    C.sbuf("l1a_kB", [128, S], BF16)
    C.sbuf("l1a_vp", [128, 16 * 130], BF16)
    C.sbuf("l1a_jt", [128, 16 * 16], F32)
    C.sbuf("l1a_tn", [128, 896], F32)
    C.sbuf("l1a_cs", [1, 512], F32)
    C.sbuf("l1a_cr", [64, 512], F32)
    C.sbuf("l1a_yb", [64, 512], F32)
    if "og" not in C.buf:
        C.sbuf("og", [128, 4 * S], BF16)
        C.sbuf("retT", [128, 16 * 512], BF16)


def _alibi_slopes16():
    start = 2.0 ** (-2.0 ** -1.0)
    return [start * start ** i for i in range(16)]


def emit_l1a_consts(C):
    nc = C.nc
    jt, tn = C.buf["l1a_jt"], C.buf["l1a_tn"]
    qA, qB, kA, kB = C.buf["l1a_qA"], C.buf["l1a_qB"], C.buf["l1a_kA"], C.buf["l1a_kB"]
    vp = C.buf["l1a_vp"]
    slopes = _alibi_slopes16()

    @C.block.gpsimd
    def _(g):
        gp = C.sem("gpw")
        for h in range(16):
            g.iota(out=jt[:, h * 16:(h + 1) * 16], pattern=[[128, 16]], base=0,
                   channel_multiplier=1, allow_small_or_imprecise_dtypes=True).then_inc(gp, 1)
            C.cnt["gpw"] += 1
        g.memset(tn[:], 0.0).then_inc(gp, 1)
        C.cnt["gpw"] += 1
        g.wait_ge(gp, C.cnt["gpw"])
        g.affine_select(out=tn[:], in_=tn[:], pattern=[[1, 896]],
                        compare_op=ALU.is_ge, fill=-1.0e38,
                        base=-384, channel_multiplier=-1).then_inc(gp, 1)
        C.cnt["gpw"] += 1
        for t_ in (qA, qB):
            g.iota(out=t_[64:65, :], pattern=[[-1, S]], base=0, channel_multiplier=0,
                   allow_small_or_imprecise_dtypes=True).then_inc(gp, 1)
            C.cnt["gpw"] += 1
        for t_ in (kA, kB):
            g.memset(t_[64:65, :], 1.0).then_inc(gp, 1)
            C.cnt["gpw"] += 1
        g.memset(vp[:].rearrange("p (t c) -> p t c", c=130)[:, :, 64:65], 1.0).then_inc(gp, 1)
        g.memset(vp[:].rearrange("p (t c) -> p t c", c=130)[:, :, 129:130], 1.0).then_inc(gp, 1)
        C.cnt["gpw"] += 2
        g.wait_ge(gp, C.cnt["gpw"])
        g.nop().then_inc(C.sem("l1ac_g"), 1)
    C.cnt["l1ac_g"] += 1

    @C.block.scalar
    def _(s):
        s.wait_ge(C.sem("l1ac_g"), C.cnt["l1ac_g"])
        last = None
        for h in range(16):
            last = s.activation(out=jt[:, h * 16:(h + 1) * 16], in_=jt[:, h * 16:(h + 1) * 16],
                                func=AF.Identity, scale=float(slopes[h]))
        last.then_inc(C.sem("l1ac_done"), 1)
    C.cnt["l1ac_done"] += 1


def emit_l1_attn(C, W):
    nc = C.nc
    xnT = C.buf["xnT"]
    qA, qB, kA, kB = C.buf["l1a_qA"], C.buf["l1a_qB"], C.buf["l1a_kA"], C.buf["l1a_kB"]
    vp = C.buf["l1a_vp"]
    jt, tn = C.buf["l1a_jt"], C.buf["l1a_tn"]
    csr, crb, yb = C.buf["l1a_cs"], C.buf["l1a_cr"], C.buf["l1a_yb"]
    wqs, wks, mvs, mps = (C.buf["l1a_wq"], C.buf["l1a_wk"], C.buf["l1a_mv"], C.buf["l1a_mp"])
    og = C.buf["og"]
    ret = C.buf["retT"]
    res = C.buf["res"]
    r0, r1 = C.buf["rot0"], C.buf["rot1"]
    slopes = _alibi_slopes16()

    @C.block.gpsimd
    def _(g):
        base = C.cnt["l1a_wfree"]
        mp_base = C.cnt["l1a_mpfree"]
        for kt in range(4):
            if mp_base > 0:
                g.wait_ge(C.sem("l1a_mpfree"), mp_base)
            g.dma_start(out=mps[:, kt * 1024:(kt + 1) * 1024],
                        in_=W["l1aux_Mproj"][128 * kt: 128 * (kt + 1), :]
                        ).then_inc(C.sem("l1a_mpld"), 16)
        for hp in range(8):
            half = (hp % 2) * 1024
            if base + hp - 1 > 0:
                g.wait_ge(C.sem("l1a_wfree"), base + hp - 1)
            g.dma_start(out=wqs[:, half:half + 1024].rearrange("p (kt n) -> p kt n", n=128),
                        in_=W["L1_attn_W_attn"].rearrange("(kt p) n -> p kt n", p=128)
                        [:, :, 128 * hp:128 * (hp + 1)]).then_inc(C.sem("l1a_wld"), 16)
            g.dma_start(out=wks[:, half:half + 1024].rearrange("p (kt n) -> p kt n", n=128),
                        in_=W["L1_attn_W_attn"].rearrange("(kt p) n -> p kt n", p=128)
                        [:, :, 1024 + 128 * hp:1024 + 128 * (hp + 1)]).then_inc(C.sem("l1a_wld"), 16)
            g.dma_start(out=mvs[:, half:half + 1024].rearrange("p (kt n) -> p kt n", n=128),
                        in_=W["l1aux_Mv"].rearrange("(kt p) n -> p kt n", p=128)
                        [:, :, 128 * hp:128 * (hp + 1)]).then_inc(C.sem("l1a_wld"), 16)
        for kt in range(4):
            g.wait_ge(C.sem("l1a_mpfree"), mp_base + 4)
            g.dma_start(out=mps[:, kt * 1024:(kt + 1) * 1024],
                        in_=W["l1aux_Mproj"][512 + 128 * kt: 512 + 128 * (kt + 1), :]
                        ).then_inc(C.sem("l1a_mpld"), 16)
    C.cnt["l1a_wld"] += 16 * 24
    C.cnt["l1a_mpld"] += 16 * 8
    wld_base = C.val("l1a_wld") - 16 * 24
    mpld_base = C.val("l1a_mpld") - 16 * 8

    for hp in range(8):
        half = (hp % 2) * 1024
        hA, hB = 2 * hp, 2 * hp + 1

        for mi in range(2):
            @C.block.tensor
            def _(t, hp=hp, mi=mi, half=half):
                if mi == 0:
                    t.wait_ge(C.sem("l1a_wld"), wld_base + 16 * (3 * hp + 2))
                    if hp == 0:
                        t.wait_ge(C.sem("xnT_done"), C.cnt["xnT_done"])
                        t.wait_ge(C.sem("l1ac_done"), C.cnt["l1ac_done"])
                wsl = wqs if mi == 0 else wks
                for tc in range(4):
                    r0.p_wait(t)
                    slot = r0.p_slot()
                    for kt2 in range(KD):
                        inst = t.matmul(slot, wsl[:, half + kt2 * 128: half + (kt2 + 1) * 128],
                                        xnT[:, kt2 * S + tc * 512: kt2 * S + tc * 512 + 512],
                                        start=(kt2 == 0), stop=(kt2 == KD - 1))
                    r0.p_done(inst, "s")

            @C.block.scalar
            def _(s, hp=hp, mi=mi, hA=hA, hB=hB):
                dstA = (qA if mi == 0 else kA)
                dstB = (qB if mi == 0 else kB)
                if hp > 0:
                    s.wait_ge(C.sem("l1a_att"), C.cnt["l1a_att"])
                for tc in range(4):
                    r0.c_wait(s)
                    slot = r0.c_slot()
                    scA = 1.0 / (8.0 * slopes[hA]) if mi == 0 else 1.0
                    scB = 1.0 / (8.0 * slopes[hB]) if mi == 0 else 1.0
                    s.activation(out=dstA[0:64, tc * 512:(tc + 1) * 512], in_=slot[0:64, :],
                                 func=AF.Identity, scale=scA)
                    inst = s.activation(out=dstB[0:64, tc * 512:(tc + 1) * 512],
                                        in_=slot[64:128, :], func=AF.Identity, scale=scB)
                    if tc < 3:
                        r0.c_done(inst, "s")
                C.bar("s", s, inst)
                r0.c_done(s.nop(), "s")
                s.nop().then_inc(C.sem("l1a_qk"), 1)
            C.cnt["l1a_qk"] += 1

        @C.block.tensor
        def _(t, hp=hp, half=half):
            t.wait_ge(C.sem("l1a_wld"), wld_base + 16 * (3 * hp + 3))
            for tt in range(NT):
                r0.p_wait(t)
                slot = r0.p_slot()
                for kt2 in range(KD):
                    inst = t.matmul(slot[:, 0:128],
                                    xnT[:, kt2 * S + tt * 128: kt2 * S + tt * 128 + 128],
                                    mvs[:, half + kt2 * 128: half + (kt2 + 1) * 128],
                                    start=(kt2 == 0), stop=(kt2 == KD - 1))
                r0.p_done(inst, "v")
            r0.self_bar_p(t)
            t.nop().then_inc(C.sem("l1a_wfree"), 1)
        C.cnt["l1a_wfree"] += 1

        @C.block.vector
        def _(v, hp=hp):
            v.wait_ge(C.sem(r1.done), C.cnt[r1.done])
            for tt in range(NT):
                r0.c_wait(v)
                slot = r0.c_slot()
                dst = vp[:].rearrange("p (t c) -> p t c", c=130)[:, tt, :]
                inst = v.tensor_copy(
                    out=dst.rearrange("p (h c) -> p h c", h=2, c=65)[:, :, 0:64],
                    in_=slot[:, 0:128].rearrange("p (h c) -> p h c", h=2))
                if tt < NT - 1:
                    r0.c_done(inst, "v")
            C.bar("v", v, inst)
            r0.c_done(v.nop(), "v")
            v.nop().then_inc(C.sem("l1a_vp"), 1)
        C.cnt["l1a_vp"] += 1

        for hh in range(2):
            qq = qA if hh == 0 else qB
            kk = kA if hh == 0 else kB
            hglob = 2 * hp + hh
            for qb in range(4):
                nkt = 4 * qb + 4

                c0 = r0.c_i
                attv_prev = C.cnt[r1.done]

                @C.block.tensor
                def _(t, hp=hp, hh=hh, qb=qb, nkt=nkt, qq=qq, kk=kk):
                    if hh == 0 and qb == 0:
                        t.wait_ge(C.sem("l1a_qk"), C.cnt["l1a_qk"])
                    for kt2 in range(nkt):
                        r0.p_wait(t)
                        inst = t.matmul(r0.p_slot(),
                                        kk[0:65, kt2 * 128:(kt2 + 1) * 128],
                                        qq[0:65, qb * 512:(qb + 1) * 512],
                                        start=True, stop=True)
                        r0.p_done(inst, "s")

                msk_base = C.cnt["l1a_msk"]

                @C.block.vector
                def _(v, qb=qb, nkt=nkt, c0=c0):
                    for kt2 in range(4 * qb, nkt):
                        rdiag = kt2 - 4 * qb
                        v.wait_ge(C.sem(r0.done), c0 + kt2 + 1)
                        v.tensor_tensor(
                            out=r0.slots[(c0 + kt2) % r0.n], in0=r0.slots[(c0 + kt2) % r0.n],
                            in1=tn[:, (3 - rdiag) * 128:(3 - rdiag) * 128 + 512],
                            op=ALU.add).then_inc(C.sem("l1a_msk"), 1)
                C.cnt["l1a_msk"] += 4

                @C.block.scalar
                def _(s, hp=hp, hh=hh, qb=qb, nkt=nkt, hglob=hglob, c0=c0,
                      attv_prev=attv_prev, msk_base=msk_base):
                    s.wait_ge(C.sem(r1.done), attv_prev)
                    inst = None
                    for kt2 in range(nkt):
                        if kt2 >= 4 * qb:
                            s.wait_ge(C.sem("l1a_msk"), msk_base + (kt2 - 4 * qb) + 1)
                        r0.c_wait(s)
                        inst = s.activation(out=ret[:, kt2 * 512:(kt2 + 1) * 512],
                                            in_=r0.c_slot(), func=AF.Exp,
                                            scale=float(slopes[hglob]),
                                            bias=jt[:, hglob * 16 + kt2: hglob * 16 + kt2 + 1])
                        if kt2 < nkt - 1:
                            r0.c_done(inst, "s")
                    C.bar("s", s, inst)
                    r0.c_done(s.nop(), "s")
                    s.nop().then_inc(C.sem("l1a_exp"), 1)
                C.cnt["l1a_exp"] += 1

                @C.block.tensor
                def _(t, hp=hp, hh=hh, qb=qb, nkt=nkt):
                    t.wait_ge(C.sem("l1a_exp"), C.cnt["l1a_exp"])
                    t.wait_ge(C.sem("l1a_vp"), C.cnt["l1a_vp"])
                    r1.p_wait(t)
                    slot = r1.p_slot()
                    for kt2 in range(nkt):
                        inst = t.matmul(slot[0:65, 0:512],
                                        vp[:, kt2 * 130 + hh * 65: kt2 * 130 + hh * 65 + 65],
                                        ret[:, kt2 * 512:(kt2 + 1) * 512],
                                        start=(kt2 == 0), stop=(kt2 == nkt - 1))
                    r1.p_done(inst, "s")

                @C.block.scalar
                def _(s, qb=qb, hglob=hglob):
                    s.wait_ge(C.sem("l1a_ogw"), C.cnt["l1a_ogw"])
                    r1.c_wait(s)
                    slot = r1.c_slot()
                    s.copy(out=yb[:], in_=slot[0:64, 0:512])
                    inst = s.copy(out=csr[:], in_=slot[64:65, 0:512])
                    C.bar("s", s, inst)
                    r1.c_done(s.nop(), "s")
                    s.nop().then_inc(C.sem("l1a_y"), 1)
                C.cnt["l1a_y"] += 1

                @C.block.vector
                def _(v, qb=qb, hglob=hglob):
                    v.wait_ge(C.sem("l1a_y"), C.cnt["l1a_y"])
                    inst = v.reciprocal(out=csr[:], in_=csr[:])
                    C.bar("v", v, inst)
                    v.nop().then_inc(C.sem("l1a_rc"), 1)
                C.cnt["l1a_rc"] += 1

                @C.block.tensor
                def _(t, qb=qb):
                    t.wait_ge(C.sem("l1a_rc"), C.cnt["l1a_rc"])
                    wait_consts(C, t)
                    r1.p_wait(t)
                    inst = t.matmul(r1.p_slot()[0:64, 0:512],
                                    C.buf["onesf_row"][0:1, 0:64], csr[0:1, :],
                                    start=True, stop=True)
                    r1.p_done(inst, "v")

                @C.block.vector
                def _(v, qb=qb, hglob=hglob):
                    r1.c_wait(v)
                    od = (hglob // 2) % 4
                    orow = (hglob % 2) * 64
                    inst = v.tensor_tensor(
                        out=og[orow:orow + 64, od * S + qb * 512: od * S + qb * 512 + 512],
                        in0=yb[:], in1=r1.c_slot()[0:64, 0:512], op=ALU.mult)
                    r1.c_done(inst, "v")
                    v.nop().then_inc(C.sem("l1a_ogw"), 1)
                C.cnt["l1a_ogw"] += 1

        @C.block.tensor
        def _(t, hp=hp):
            t.nop().then_inc(C.sem("l1a_att"), 1)
        C.cnt["l1a_att"] += 1

        if hp in (3, 7):
            ps = hp // 4
            @C.block.tensor
            def _(t, hp=hp, ps=ps):
                t.wait_ge(C.sem("l1a_ogw"), C.cnt["l1a_ogw"])
                for tt in range(NT):
                    r1.p_wait(t)
                    slot = r1.p_slot()
                    for nc2 in range(2):
                        sl = slot[:, nc2 * 512:(nc2 + 1) * 512]
                        for kt2 in range(4):
                            t.wait_ge(C.sem("l1a_mpld"), mpld_base + 16 * (4 * ps + kt2 + 1))
                            inst = t.matmul(sl, og[:, kt2 * S + tt * 128: kt2 * S + tt * 128 + 128],
                                            mps[:, kt2 * 1024 + nc2 * 512:
                                                kt2 * 1024 + nc2 * 512 + 512],
                                            start=(kt2 == 0), stop=(kt2 == 3))
                    r1.p_done(inst, "v")
                t.nop().then_inc(C.sem("l1a_mpfree"), 4)
            C.cnt["l1a_mpfree"] += 4

            @C.block.vector
            def _(v, hp=hp):
                for tt in range(NT):
                    r1.c_wait(v)
                    slot = r1.c_slot()
                    v.tensor_tensor(out=res[:, tt * D: tt * D + 512],
                                    in0=res[:, tt * D: tt * D + 512], in1=slot[:, 0:512],
                                    op=ALU.add)
                    inst = v.tensor_tensor(out=res[:, tt * D + 512: tt * D + 1024],
                                           in0=res[:, tt * D + 512: tt * D + 1024],
                                           in1=slot[:, 512:1024], op=ALU.add)
                    if tt < NT - 1:
                        r1.c_done(inst, "v")
                C.bar("v", v, inst)
                r1.c_done(v.nop(), "v")
                v.nop().then_inc(C.sem("res_upd"), 1)
            C.cnt["res_upd"] += 1



F16 = mybir.dt.float16
ROW = S + 1
DIN = 2816
NJ = DIN // 128


def alloc_l1_ffn(C):
    C.sbuf("xi", [128, NT * D], F16)
    C.sbuf("xiT", [128, KD * ROW], BF16)
    scr = C.buf["res"]
    C.buf["scr"] = scr
    C.buf["prod"] = scr[:, 0:11264].bitcast(BF16)
    C.buf["srow"] = scr[:, 11264:13312]
    C.buf["rsqb"] = scr[:, 13312:15360]
    C.buf["sqs"] = scr[:, 15360:16384].bitcast(BF16)
    big = C.sbuf("l1f_big", [128, 11264], BF16)
    C.buf["l1f_q"] = big[:, 0:2048]
    C.buf["l1f_k"] = big[:, 2048:4096]
    C.buf["l1f_sw"] = big[:, 4096:6144]
    C.buf["l1f_v"] = big[:, 6144:8192]
    C.buf["l1f_at"] = big[:, 8192:9216]
    C.buf["l1f_aT"] = big[:, 9216:10240]
    C.sbuf("l1f_rs", [128, 8], F32)
    if "og" not in C.buf:
        w = C.sbuf("l1fw", [128, 16384], BF16)
        C.buf["og"] = w[:, 0:8192]
        C.buf["l1f_wqkv"] = w[:, 8192:14336]
        C.buf["l1f_w1"] = w[:, 14336:16384]
    else:
        C.sbuf("l1f_wqkv", [128, 2 * 3072], BF16)
        C.sbuf("l1f_w1", [128, 2048], BF16)
    if "iotaT" not in C.buf:
        C.buf["iotaT"] = big[:, 0:4096].bitcast(F32)
        C.buf["ropeC"] = big[:, 4096:6144]
        C.buf["ropeS"] = big[:, 6144:8192]
        C.sbuf("ifcol", [128, 1], F32)
    C.sbuf("l1f_cq", [128, S], BF16)
    C.sbuf("l1f_sq", [128, S], BF16)
    C.sbuf("l1f_ck", [128, S], BF16)
    C.sbuf("l1f_sk", [128, S], BF16)
    xns = C.buf["xnstage"]
    C.buf["l1f_msk"] = xns[:, 0:1536]
    C.buf["l1f_sel"] = xns[:, 1536:1600]
    C.buf["l1f_sm"] = xns[:, 1600:1724].bitcast(F32)
    C.buf["l1f_qs"] = xns[:, 1600:1604].bitcast(F32)
    C.buf["l1f_gb"] = xns[:, 1604:1636].bitcast(F32)
    C.buf["l1f_b1"] = xns[:, 1636:1724].bitcast(F32)
    if "bn6" not in C.buf:
        C.buf["bn6"] = xns[:, 1724:2492].bitcast(F32)
        C.buf["mv16"] = xns[:, 2492:2556].bitcast(F32)
        C.buf["nm16"] = xns[:, 2556:2588].bitcast(F32)
        C.buf["rs16"] = xns[:, 2588:2620].bitcast(F32)
        C.buf["onesf_row"] = xns[:, 2620:2876].bitcast(F32)
    C.buf["l1f_b2"] = xns[0:1, 3072:4096]


def emit_l1f_consts(C):
    nc = C.nc
    iotaT = C.buf["iotaT"]
    ropeC, ropeS = C.buf["ropeC"], C.buf["ropeS"]
    cq, sq_, ck, sk = C.buf["l1f_cq"], C.buf["l1f_sq"], C.buf["l1f_ck"], C.buf["l1f_sk"]
    msk, sel = C.buf["l1f_msk"], C.buf["l1f_sel"]
    og = C.buf["og"]
    tA = og[:, 0:4096].bitcast(F32)
    rsc = C.buf["l1f_rs"]

    xiT = C.buf["xiT"]

    @C.block.gpsimd
    def _(g):
        gp = C.sem("gpw")
        g.memset(xiT[:].rearrange("p (kd c) -> p kd c", c=ROW)[:, :, 0:1],
                 0.0).then_inc(gp, 1)
        g.memset(sel[:], 0.0).then_inc(gp, 1)
        C.cnt["gpw"] += 2
        g.wait_ge(gp, C.cnt["gpw"])
        g.memset(sel[0:64, 0:1], 1.0).then_inc(gp, 1)
        g.memset(sel[64:128, 32:33], 1.0).then_inc(gp, 1)
        g.memset(msk[:], 0.0).then_inc(gp, 1)
        C.cnt["gpw"] += 3
        g.wait_ge(gp, C.cnt["gpw"])
        for it in range(2):
            g.affine_select(out=msk[:, it * 768:(it + 1) * 768],
                            in_=msk[:, it * 768:(it + 1) * 768],
                            pattern=[[-1, 768]], compare_op=ALU.is_ge, fill=-1.0e38,
                            base=256 + 128 * it, channel_multiplier=1).then_inc(gp, 1)
            C.cnt["gpw"] += 1
            g.wait_ge(gp, C.cnt["gpw"])
            g.affine_select(out=msk[:, it * 768:(it + 1) * 768],
                            in_=msk[:, it * 768:(it + 1) * 768],
                            pattern=[[1, 768]], compare_op=ALU.is_ge, fill=-1.0e38,
                            base=-128 * it, channel_multiplier=-1).then_inc(gp, 1)
            C.cnt["gpw"] += 1
            g.wait_ge(gp, C.cnt["gpw"])
        for qrt in range(4):
            g.iota(out=rsc[qrt * 32:(qrt + 1) * 32, 0:1], pattern=[[0, 1]], base=0,
                   channel_multiplier=1, allow_small_or_imprecise_dtypes=True).then_inc(gp, 1)
            C.cnt["gpw"] += 1
        g.wait_ge(gp, C.cnt["gpw"])
        g.nop().then_inc(C.sem("l1fc_g"), 1)
    C.cnt["l1fc_g"] += 1

    @C.block.scalar
    def _(s):
        s.wait_ge(C.sem("l1fc_g"), C.cnt["l1fc_g"])
        s.wait_ge(C.sem("l0c_done"), C.cnt["l0c_done"])
        inst = s.activation(out=rsc[:, 1:2], in_=rsc[:, 0:1], func=AF.Ln,
                            scale=2.0 / 89.6, bias=C.constcol(25.6 / 89.6))
        C.bar("s", s, inst)
        s.nop().then_inc(C.sem("l1fc_s1"), 1)
    C.cnt["l1fc_s1"] += 1

    @C.block.vector
    def _(v):
        v.wait_ge(C.sem("l1fc_s1"), C.cnt["l1fc_s1"])
        inst = v.tensor_scalar(out=tA[:], in0=iotaT[:], scalar1=-1024.0, scalar2=1.0 / 128.0,
                               op0=ALU.add, op1=ALU.mult)
        C.bar("v", v, inst)
        inst = v.tensor_scalar(out=tA[:], in0=tA[:], scalar1=rsc[:, 1:2], scalar2=None,
                               op0=ALU.mult, op1=ALU.bypass)
        C.bar("v", v, inst)
        v.nop().then_inc(C.sem("l1fc_v1"), 1)
    C.cnt["l1fc_v1"] += 1

    @C.block.scalar
    def _(s):
        s.wait_ge(C.sem("l1fc_v1"), C.cnt["l1fc_v1"])
        s.activation(out=cq[:], in_=tA[:], func=AF.Exp)
        inst = s.activation(out=ck[:], in_=tA[:], func=AF.Exp, scale=-1.0)
        C.bar("s", s, inst)
        s.nop().then_inc(C.sem("l1fc_s2"), 1)
    C.cnt["l1fc_s2"] += 1

    @C.block.vector
    def _(v):
        v.wait_ge(C.sem("l1fc_s2"), C.cnt["l1fc_s2"])
        inst = v.tensor_tensor(out=sq_[:], in0=ropeS[:], in1=cq[:], op=ALU.mult)
        C.bar("v", v, inst)
        v.tensor_scalar(out=sq_[:], in0=sq_[:], scalar1=8.0, scalar2=None,
                        op0=ALU.mult, op1=ALU.bypass)
        inst = v.tensor_tensor(out=sk[:], in0=ropeS[:], in1=ck[:], op=ALU.mult)
        C.bar("v", v, inst)
        inst = v.tensor_tensor(out=cq[:], in0=ropeC[:], in1=cq[:], op=ALU.mult)
        C.bar("v", v, inst)
        v.tensor_scalar(out=cq[:], in0=cq[:], scalar1=8.0, scalar2=None,
                        op0=ALU.mult, op1=ALU.bypass)
        inst = v.tensor_tensor(out=ck[:], in0=ropeC[:], in1=ck[:], op=ALU.mult)
        C.bar("v", v, inst)
        v.nop().then_inc(C.sem("l1fc_done"), 1)
    C.cnt["l1fc_done"] += 1


def emit_t2fm_plain(C, src_name, dst_name, done_sem):
    src_t = C.buf[src_name]
    dst = C.buf[dst_name]
    rotT = C.buf["rotT"]
    psTf = C.buf["psT"][:].bitcast(F16)

    for g in range(8):
        @C.block.tensor
        def _(t, g=g):
            wait_consts(C, t)
            for kg in range(2):
                rotT.p_wait(t)
                ofs = (rotT.p_i % rotT.n) * 1024
                last = None
                for i in range(4):
                    kd = kg * 4 + i
                    for j in range(2):
                        tt = 2 * g + j
                        last = t.transpose(
                            psTf[:, ofs + i * 256 + j * 128: ofs + i * 256 + (j + 1) * 128],
                            src_t[:, tt * D + kd * 128: tt * D + kd * 128 + 128],
                            C.buf["ident16"][:])
                rotT.p_done(last, "v")
            rotT.self_bar_p(t)

        @C.block.vector
        def _(v, g=g):
            for kg in range(2):
                rotT.c_wait(v)
                ofs = (rotT.c_i % rotT.n) * 1024
                for i in range(4):
                    kd = kg * 4 + i
                    inst = v.tensor_copy(out=dst[:, kd * ROW + 1 + g * 256:
                                                 kd * ROW + 1 + (g + 1) * 256],
                                         in_=psTf[:, ofs + i * 256: ofs + (i + 1) * 256])
                if g == 7 and kg == 1:
                    C.bar("v", v, inst)
                    rotT.c_done(v.nop())
                else:
                    rotT.c_done(inst)

    @C.block.vector
    def _(v):
        v.nop().then_inc(C.sem(done_sem), 1)
    C.cnt[done_sem] += 1


def emit_l1_ffn(C, W, x_scr):
    nc = C.nc
    res, xi = C.buf["res"], C.buf["xi"]
    nm, rs = C.buf["nm16"], C.buf["rs16"]

    emit_ln_stats(C)

    @C.block.scalar
    def _(s):
        s.wait_ge(C.sem("ln_nmrs"), C.cnt["ln_nmrs"])
        s.wait_ge(C.sem("ln_rs"), C.cnt["ln_rs"])
        s.wait_ge(C.sem("l1f_xif"), C.cnt["l1f_xif"])
        last = None
        for i in range(NT):
            last = s.activation(out=xi[:, i * D:(i + 1) * D], in_=res[:, i * D:(i + 1) * D],
                                func=AF.Identity, bias=nm[:, i:i + 1], scale=rs[:, i:i + 1])
        last.then_inc(C.sem("l1f_xi"), 1)
    C.cnt["l1f_xi"] += 1

    @C.block.sync
    def _(sy):
        sy.wait_ge(C.sem("l1f_xi"), C.cnt["l1f_xi"])
        sy.wait_ge(C.sem("l1f_rld"), C.cnt["l1f_rld"])
        sy.dma_start(out=x_scr[:].rearrange("(t p) d -> p t d", p=128),
                     in_=res[:].rearrange("p (t d) -> p t d", d=D)
                     ).then_inc(C.sem("l1f_spl"), 16)
    C.cnt["l1f_spl"] += 16

    for li in range(L1F_LAYERS):
        emit_l1f_layer(C, W, li)

    @C.block.sync
    def _(sy):
        sy.wait_ge(C.sem("l1f_spl"), C.cnt["l1f_spl"])
        sy.wait_ge(C.sem("l1f_upd"), C.cnt["l1f_upd"])
        sy.dma_start(out=res[:].rearrange("p (t d) -> p t d", d=D),
                     in_=x_scr[:].rearrange("(t p) d -> p t d", p=128)
                     ).then_inc(C.sem("l1f_rld"), 16)
    C.cnt["l1f_rld"] += 16

    @C.block.vector
    def _(v):
        v.wait_ge(C.sem("l1f_rld"), C.cnt["l1f_rld"])
        v.wait_ge(C.sem("l1f_upd"), C.cnt["l1f_upd"])
        inst = None
        for i in range(NT):
            inst = v.tensor_tensor(out=res[:, i * D:(i + 1) * D],
                                   in0=res[:, i * D:(i + 1) * D],
                                   in1=xi[:, i * D:(i + 1) * D], op=ALU.add)
        C.bar("v", v, inst)
        v.nop().then_inc(C.sem("res_upd"), 1)
        v.nop().then_inc(C.sem("l1f_xif"), 1)
    C.cnt["res_upd"] += 1
    C.cnt["l1f_xif"] += 1


def emit_l1f_layer(C, W, li):
    nc = C.nc
    xi, xiT = C.buf["xi"], C.buf["xiT"]
    prod, srow, rsqb, sqs = C.buf["prod"], C.buf["srow"], C.buf["rsqb"], C.buf["sqs"]
    qb_, kb_, vb_ = C.buf["l1f_q"], C.buf["l1f_k"], C.buf["l1f_v"]
    swb, atb, aTb = C.buf["l1f_sw"], C.buf["l1f_at"], C.buf["l1f_aT"]
    rsc = C.buf["l1f_rs"]
    big = C.buf["l1f_big"]
    wqkv, w1s = C.buf["l1f_wqkv"], C.buf["l1f_w1"]
    cq, sq_, ck, sk = C.buf["l1f_cq"], C.buf["l1f_sq"], C.buf["l1f_ck"], C.buf["l1f_sk"]
    msk, sel = C.buf["l1f_msk"], C.buf["l1f_sel"]
    qsc, gbc = C.buf["l1f_qs"], C.buf["l1f_gb"]
    og = C.buf["og"]
    r0, r1, rotT = C.buf["rot0"], C.buf["rot1"], C.buf["rotT"]

    @C.block.tensor
    def _(t):
        t.wait_ge(C.sem("l1f_xi" if li == 0 else "l1f_upd"),
                  C.cnt["l1f_xi" if li == 0 else "l1f_upd"])
    emit_t2fm_plain(C, "xi", "xiT", "l1f_xT")

    slab_base = C.cnt["l1f_wfree"]
    slab_n = [0]
    wld = (C.val("l1f_wldA"), C.val("l1f_wldB"))
    wold = (C.val("l1f_woldA"), C.val("l1f_woldB"))

    def emit_slab_load(kind, idx):
        n = slab_n[0]
        slab_n[0] = n + 1

        @C.block.gpsimd
        def _(g, kind=kind, idx=idx, n=n):
            if slab_base + n - 1 > 0:
                g.wait_ge(C.sem("l1f_wfree"), slab_base + n - 1)
            half = (n % 2) * 3072
            if kind == "qkv":
                hp = idx
                sm_ = C.sem("l1f_wldA" if n % 2 == 0 else "l1f_wldB")
                srcw = W["L1_ffn_Wqkv"][li].rearrange("(kt p) n -> p kt n", p=128)
                wv = wqkv[:, half:half + 3072].rearrange("p (kt ci nb) -> p kt ci nb",
                                                         ci=3, nb=128)
                for ci in range(3):
                    g.dma_start(out=wv[:, :, ci, :],
                                in_=srcw[:, :, 1024 * ci + 128 * hp:
                                         1024 * ci + 128 * (hp + 1)]
                                ).then_inc(sm_, 16)
            else:
                ps, sub = idx
                sm_ = C.sem("l1f_woldA" if n % 2 == 0 else "l1f_woldB")
                wo_src = W["L1_ffn_Wo"][li].rearrange("(kt p) n -> p kt n", p=128)
                g.dma_start(out=wqkv[:, half:half + 2048]
                            .rearrange("p (k2 nn) -> p k2 nn", nn=1024),
                            in_=wo_src[:, 4 * ps + 2 * sub: 4 * ps + 2 * sub + 2, :]
                            ).then_inc(sm_, 16)
        if kind == "qkv":
            C.cnt["l1f_wldA" if n % 2 == 0 else "l1f_wldB"] += 48
        else:
            C.cnt["l1f_woldA" if n % 2 == 0 else "l1f_woldB"] += 16

    emit_slab_load("qkv", 0)
    emit_slab_load("qkv", 1)

    @C.block.gpsimd
    def _(g):
        g.wait_ge(C.sem("l1f_smfree"), C.cnt["l1f_smfree"])
        g.dma_start(out=C.buf["l1f_sm"][:], in_=W["l1aux_sm"][li]
                    ).then_inc(C.sem("l1f_smld"), 16)
        g.dma_start(out=C.buf["l1f_b2"][:], in_=W["L1_ffn_b2"][li][None, :]
                    ).then_inc(C.sem("l1f_smld"), 16)
    C.cnt["l1f_smld"] += 16 * 2
    smld = C.val("l1f_smld")

    for hp in range(8):
        half = (hp % 2) * 3072

        for mi in range(3):
            @C.block.tensor
            def _(t, hp=hp, mi=mi, half=half):
                if mi == 0:
                    t.wait_ge(C.sem("l1f_wldA" if hp % 2 == 0 else "l1f_wldB"),
                              wld[hp % 2] + 48 * (hp // 2 + 1))
                    if hp == 0:
                        t.wait_ge(C.sem("l1f_xT"), C.cnt["l1f_xT"])
                for tc in range(4):
                    r0.p_wait(t)
                    slot = r0.p_slot()
                    for kt2 in range(KD):
                        off = 1 if kt2 < 4 else 0
                        inst = t.matmul(slot,
                                        wqkv[:, half + kt2 * 384 + mi * 128:
                                             half + kt2 * 384 + (mi + 1) * 128],
                                        xiT[:, kt2 * ROW + off + tc * 512:
                                            kt2 * ROW + off + tc * 512 + 512],
                                        start=(kt2 == 0), stop=(kt2 == KD - 1))
                    r0.p_done(inst, "s" if mi < 2 else "v")
                if mi == 2:
                    r0.self_bar_p(t)
                    t.nop().then_inc(C.sem("l1f_wfree"), 1)
            if mi == 2:
                C.cnt["l1f_wfree"] += 1

            if mi < 2:
                @C.block.scalar
                def _(s, hp=hp, mi=mi):
                    dst = qb_ if mi == 0 else kb_
                    if hp == 0 and mi == 0:
                        s.wait_ge(C.sem("l1fc_done"), C.cnt["l1fc_done"])
                    if hp > 0 and mi == 0:
                        s.wait_ge(C.sem("l1f_att"), C.cnt["l1f_att"])
                    for tc in range(4):
                        r0.c_wait(s)
                        inst = s.copy(out=dst[:, tc * 512:(tc + 1) * 512], in_=r0.c_slot())
                        if tc < 3:
                            r0.c_done(inst, "s")
                    C.bar("s", s, inst)
                    r0.c_done(s.nop(), "s")
                    s.nop().then_inc(C.sem("l1f_qkraw"), 1)
                C.cnt["l1f_qkraw"] += 1
            else:
                @C.block.vector
                def _(v, hp=hp):
                    if hp == 0:
                        v.wait_ge(C.sem("l1fc_done"), C.cnt["l1fc_done"])
                    if hp > 0:
                        v.wait_ge(C.sem("l1f_att"), C.cnt["l1f_att"])
                    for tc in range(4):
                        r0.c_wait(v)
                        inst = v.tensor_copy(out=swb[:, tc * 512:(tc + 1) * 512],
                                             in_=r0.c_slot())
                        if tc < 3:
                            r0.c_done(inst, "v")
                    C.bar("v", v, inst)
                    r0.c_done(v.nop(), "v")
                    v.nop().then_inc(C.sem("l1f_vf"), 1)
                C.cnt["l1f_vf"] += 1

        @C.block.tensor
        def _(t, hp=hp):
            t.wait_ge(C.sem("l1f_vf"), C.cnt["l1f_vf"])
            for g2 in range(2):
                rotT.p_wait(t)
                slot = rotT.p_slot()
                last = None
                for j in range(8):
                    tt = g2 * 8 + j
                    last = t.transpose(slot[:, j * 128:(j + 1) * 128],
                                       swb[:, tt * 128:(tt + 1) * 128], C.buf["ident"][:])
                rotT.p_done(last, "v")
            rotT.self_bar_p(t)
            t.nop().then_inc(C.sem("l1f_vT"), 1)
        C.cnt["l1f_vT"] += 1

        @C.block.vector
        def _(v, hp=hp):
            v.wait_ge(C.sem(r1.done), C.cnt[r1.done])
            for g2 in range(2):
                rotT.c_wait(v)
                inst = v.tensor_copy(out=vb_[:, g2 * 1024:(g2 + 1) * 1024],
                                     in_=rotT.c_slot()[:, 0:1024])
                if g2 < 1:
                    rotT.c_done(inst)
            C.bar("v", v, inst)
            rotT.c_done(v.nop())
            v.nop().then_inc(C.sem("l1f_vtm"), 1)
        C.cnt["l1f_vtm"] += 1

        for qk in range(2):
            buf = qb_ if qk == 0 else kb_
            sumdst = srow if qk == 0 else rsqb

            @C.block.vector
            def _(v, hp=hp, qk=qk, buf=buf):
                if qk == 0:
                    v.wait_ge(C.sem("l1f_qkraw"), C.cnt["l1f_qkraw"])
                    v.wait_ge(C.sem("l1f_spl"), C.cnt["l1f_spl"])
                else:
                    r0.self_bar_p(v)
                inst = v.tensor_tensor(out=sqs[:], in0=buf[:], in1=buf[:], op=ALU.mult)
                inst.then_inc(C.sem("l1f_sq"), 1)
            C.cnt["l1f_sq"] += 1

            @C.block.tensor
            def _(t, hp=hp, qk=qk):
                t.wait_ge(C.sem("l1f_sq"), C.cnt["l1f_sq"])
                wait_consts(C, t)
                for ch in range(4):
                    r0.p_wait(t)
                    inst = t.matmul(r0.p_slot()[0:64, :], sel[:],
                                    sqs[:, ch * 512:(ch + 1) * 512], start=True, stop=True)
                    r0.p_done(inst, "s")

            @C.block.scalar
            def _(s, hp=hp, qk=qk, sumdst=sumdst):
                for ch in range(4):
                    r0.c_wait(s)
                    inst = s.copy(out=sumdst[0:64, ch * 512:(ch + 1) * 512],
                                  in_=r0.c_slot()[0:64, :])
                    if ch < 3:
                        r0.c_done(inst, "s")
                C.bar("s", s, inst)
                r0.c_done(s.nop(), "s")
                s.nop().then_inc(C.sem("l1f_sums"), 1)
            C.cnt["l1f_sums"] += 1

        @C.block.vector
        def _(v, hp=hp):
            v.wait_ge(C.sem("l1f_sums"), C.cnt["l1f_sums"])
            v.tensor_scalar(out=srow[0:64, :], in0=srow[0:64, :], scalar1=1e-24,
                            scalar2=None, op0=ALU.max, op1=ALU.bypass)
            inst = v.tensor_scalar(out=rsqb[0:64, :], in0=rsqb[0:64, :], scalar1=1e-24,
                                   scalar2=None, op0=ALU.max, op1=ALU.bypass)
            inst.then_inc(C.sem("l1f_cl"), 1)
        C.cnt["l1f_cl"] += 1

        @C.block.scalar
        def _(s, hp=hp):
            s.wait_ge(C.sem("l1f_cl"), C.cnt["l1f_cl"])
            for dst in (srow, rsqb):
                inst = s.activation(out=dst[0:64, :], in_=dst[0:64, :], func=AF.Ln)
                C.bar("s", s, inst)
                inst = s.activation(out=dst[0:64, :], in_=dst[0:64, :], func=AF.Exp,
                                    scale=-0.5)
                C.bar("s", s, inst)
            s.nop().then_inc(C.sem("l1f_rsq"), 1)
        C.cnt["l1f_rsq"] += 1

        for qk in range(2):
            buf = qb_ if qk == 0 else kb_
            sumdst = srow if qk == 0 else rsqb
            ctab = cq if qk == 0 else ck
            stab = sq_ if qk == 0 else sk

            @C.block.tensor
            def _(t, hp=hp, qk=qk, sumdst=sumdst):
                t.wait_ge(C.sem("l1f_rsq"), C.cnt["l1f_rsq"])
                for cc in range(4):
                    cs = slice(cc * 512, (cc + 1) * 512)
                    r0.p_wait(t)
                    slot = r0.p_slot()
                    t.matmul(slot[0:64, 0:512],
                             C.buf["onesf_row"][0:1, 0:64],
                             sumdst[0:1, cs], start=True, stop=True)
                    inst = t.matmul(slot[64:128, 0:512],
                                    C.buf["onesf_row"][32:33, 0:64],
                                    sumdst[32:33, cs],
                                    start=True, stop=True)
                    r0.p_done(inst, "v")

            @C.block.vector
            def _(v, hp=hp, qk=qk, buf=buf, ctab=ctab, stab=stab):
                if qk == 0:
                    if hp == 0:
                        v.wait_ge(C.sem("l1fc_done"), C.cnt["l1fc_done"])
                        v.wait_ge(C.sem("l1f_smld"), smld)
                    v.wait_ge(C.sem("l1f_vtm"), C.cnt["l1f_vtm"])
                inst = None
                for cc in range(4):
                    cs = slice(cc * 512, (cc + 1) * 512)
                    r0.c_wait(v)
                    inst = v.scalar_tensor_tensor(out=buf[:, cs], in0=buf[:, cs],
                                                  scalar=qsc[:, qk:qk + 1],
                                                  in1=r0.c_slot()[:, 0:512],
                                                  op0=ALU.mult, op1=ALU.mult)
                    if cc < 3:
                        r0.c_done(inst, "v")
                C.bar("v", v, inst)
                r0.c_done(v.nop(), "v")
                for blk in range(2):
                    b0 = blk * 64
                    v.tensor_copy(out=swb[b0:b0 + 32, :], in_=buf[b0 + 32:b0 + 64, :])
                    inst = v.tensor_copy(out=swb[b0 + 32:b0 + 64, :], in_=buf[b0:b0 + 32, :])
                C.bar("v", v, inst)
                inst = v.tensor_tensor(out=buf[:], in0=buf[:], in1=ctab[:], op=ALU.mult)
                inst2 = v.tensor_tensor(out=swb[:], in0=swb[:], in1=stab[:], op=ALU.mult)
                C.bar("v", v, inst2)
                inst = v.tensor_tensor(out=buf[:], in0=buf[:], in1=swb[:], op=ALU.add)
                inst.then_inc(C.sem("l1f_rope"), 1)
            C.cnt["l1f_rope"] += 1

        for hh in range(2):
            hoff = hh * 64
            hglob = 2 * hp + hh
            for wb in range(8):
                njt = 2 if wb == 0 else 4
                jt0 = 0 if wb == 0 else (wb - 1) * 2
                c0 = r0.c_i

                @C.block.tensor
                def _(t, hp=hp, hh=hh, wb=wb, njt=njt, jt0=jt0, hoff=hoff):
                    if hh == 0 and wb == 0:
                        t.wait_ge(C.sem("l1f_rope"), C.cnt["l1f_rope"])
                    for it in range(2):
                        i0 = wb * 256 + it * 128
                        r0.p_wait(t)
                        inst = t.matmul(r0.p_slot()[:, 0:njt * 128],
                                        qb_[hoff:hoff + 64, i0:i0 + 128],
                                        kb_[hoff:hoff + 64, jt0 * 128: jt0 * 128 + njt * 128],
                                        start=True, stop=True)
                        r0.p_done(inst, "s")

                @C.block.vector
                def _(v, wb=wb, njt=njt, c0=c0):
                    inst = None
                    for it in range(2):
                        v.wait_ge(C.sem(r0.done), c0 + it + 1)
                        slot = r0.slots[(c0 + it) % r0.n]
                        moff = 256 if wb == 0 else 0
                        inst = v.tensor_tensor(out=slot[:, 0:njt * 128],
                                               in0=slot[:, 0:njt * 128],
                                               in1=msk[:, it * 768 + moff:
                                                       it * 768 + moff + njt * 128],
                                               op=ALU.add)
                        C.bar("v", v, inst)
                        inst = v.tensor_reduce(out=rsc[:, 2 + it:3 + it],
                                               in_=slot[:, 0:njt * 128],
                                               axis=mybir.AxisListType.X, op=ALU.max,
                                               negate=True)
                    C.bar("v", v, inst)
                    v.nop().then_inc(C.sem("l1f_mx"), 1)
                C.cnt["l1f_mx"] += 1

                @C.block.scalar
                def _(s, wb=wb, njt=njt):
                    s.wait_ge(C.sem("l1f_mx"), C.cnt["l1f_mx"])
                    s.wait_ge(C.sem("l1f_aT"), C.cnt["l1f_aT"])
                    for it in range(2):
                        r0.c_wait(s)
                        inst = s.activation(out=atb[:, it * 512: it * 512 + njt * 128],
                                            in_=r0.c_slot()[:, 0:njt * 128], func=AF.Exp,
                                            bias=rsc[:, 2 + it:3 + it],
                                            accum_out=rsc[:, 4 + it:5 + it])
                        if it == 0:
                            r0.c_done(inst, "s")
                    C.bar("s", s, inst)
                    r0.c_done(s.nop(), "s")
                    inst = s.activation(out=rsc[:, 4:6], in_=rsc[:, 4:6], func=AF.Ln)
                    C.bar("s", s, inst)
                    inst = s.activation(out=rsc[:, 4:6], in_=rsc[:, 4:6], func=AF.Exp,
                                        scale=-1.0)
                    C.bar("s", s, inst)
                    s.nop().then_inc(C.sem("l1f_ex"), 1)
                C.cnt["l1f_ex"] += 1

                @C.block.vector
                def _(v, wb=wb, njt=njt):
                    v.wait_ge(C.sem("l1f_ex"), C.cnt["l1f_ex"])
                    inst = None
                    for it in range(2):
                        inst = v.tensor_scalar(out=atb[:, it * 512: it * 512 + njt * 128],
                                               in0=atb[:, it * 512: it * 512 + njt * 128],
                                               scalar1=rsc[:, 4 + it:5 + it], scalar2=None,
                                               op0=ALU.mult, op1=ALU.bypass)
                    C.bar("v", v, inst)
                    v.nop().then_inc(C.sem("l1f_an"), 1)
                C.cnt["l1f_an"] += 1

                @C.block.tensor
                def _(t, wb=wb, njt=njt):
                    t.wait_ge(C.sem("l1f_an"), C.cnt["l1f_an"])
                    rotT.p_wait(t)
                    slot = rotT.p_slot()
                    last = None
                    for jc in range(njt):
                        for it in range(2):
                            last = t.transpose(slot[:, (jc * 2 + it) * 128:
                                                    (jc * 2 + it + 1) * 128],
                                               atb[:, it * 512 + jc * 128:
                                                   it * 512 + (jc + 1) * 128],
                                               C.buf["ident"][:])
                    rotT.p_done(last, "v")
                    rotT.self_bar_p(t)
                    t.nop().then_inc(C.sem("l1f_aT"), 1)
                C.cnt["l1f_aT"] += 1

                @C.block.vector
                def _(v, wb=wb, njt=njt):
                    rotT.c_wait(v)
                    inst = v.tensor_copy(out=aTb[:, 0:njt * 256],
                                         in_=rotT.c_slot()[:, 0:njt * 256])
                    C.bar("v", v, inst)
                    rotT.c_done(v.nop())
                    v.nop().then_inc(C.sem("l1f_aTe"), 1)
                C.cnt["l1f_aTe"] += 1

                @C.block.tensor
                def _(t, hp=hp, hh=hh, wb=wb, njt=njt, jt0=jt0, hoff=hoff):
                    t.wait_ge(C.sem("l1f_aTe"), C.cnt["l1f_aTe"])
                    t.wait_ge(C.sem("l1f_vtm"), C.cnt["l1f_vtm"])
                    r1.p_wait(t)
                    slot = r1.p_slot()
                    for jc in range(njt):
                        inst = t.matmul(slot[0:64, 0:256],
                                        vb_[:, (jt0 + jc) * 128 + hoff:
                                            (jt0 + jc) * 128 + hoff + 64],
                                        aTb[:, jc * 256:(jc + 1) * 256],
                                        start=(jc == 0), stop=(jc == njt - 1))
                    r1.p_done(inst, "v")

                @C.block.vector
                def _(v, hp=hp, hh=hh, wb=wb, hglob=hglob, hoff=hoff):
                    r1.c_wait(v)
                    slot = r1.c_slot()
                    od = (hglob // 2) % 4
                    orow = (hglob % 2) * 64
                    inst = v.tensor_copy(
                        out=og[orow:orow + 64, od * S + wb * 256: od * S + wb * 256 + 256],
                        in_=slot[0:64, 0:256])
                    r1.c_done(inst, "v")

        @C.block.tensor
        def _(t, hp=hp):
            r0.self_bar_p(t)
            t.nop().then_inc(C.sem("l1f_att"), 1)
        C.cnt["l1f_att"] += 1

        if hp == 0:
            emit_slab_load("qkv", 2)
        elif hp == 1:
            emit_slab_load("qkv", 3)
        elif hp == 3:
            emit_slab_load("wo", (0, 0))
            emit_slab_load("wo", (0, 1))
            emit_slab_load("qkv", 4)
            emit_slab_load("qkv", 5)
        elif hp == 4:
            emit_slab_load("qkv", 6)
        elif hp == 5:
            emit_slab_load("qkv", 7)
        elif hp == 7:
            emit_slab_load("wo", (1, 0))
            emit_slab_load("wo", (1, 1))

        if hp in (3, 7):
            ps = hp // 4
            for sub in range(2):
                half = sub * 3072

                @C.block.tensor
                def _(t, ps=ps, sub=sub, half=half):
                    t.wait_ge(C.sem("l1f_woldA" if sub == 0 else "l1f_woldB"),
                              wold[sub] + 16 * (ps + 1))
                    r1.free_count_wait(t)
                    for tt in range(NT):
                        r1.p_wait(t)
                        slot = r1.p_slot()
                        for nc2 in range(2):
                            sl = slot[:, nc2 * 512:(nc2 + 1) * 512]
                            for k2 in range(2):
                                od = 2 * sub + k2
                                inst = t.matmul(sl,
                                                og[:, od * S + tt * 128: od * S + tt * 128 + 128],
                                                wqkv[:, half + k2 * 1024 + nc2 * 512:
                                                     half + k2 * 1024 + nc2 * 512 + 512],
                                                start=(k2 == 0), stop=(k2 == 1))
                        r1.p_done(inst, "v")
                    r1.self_bar_p(t)
                    t.nop().then_inc(C.sem("l1f_wfree"), 1)
                C.cnt["l1f_wfree"] += 1

                @C.block.vector
                def _(v, ps=ps, sub=sub):
                    inst = None
                    for tt in range(NT):
                        r1.c_wait(v)
                        slot = r1.c_slot()
                        v.tensor_tensor(out=xi[:, tt * D: tt * D + 512],
                                        in0=xi[:, tt * D: tt * D + 512], in1=slot[:, 0:512],
                                        op=ALU.add)
                        inst = v.tensor_tensor(out=xi[:, tt * D + 512: tt * D + 1024],
                                               in0=xi[:, tt * D + 512: tt * D + 1024],
                                               in1=slot[:, 512:1024], op=ALU.add)
                        if tt < NT - 1:
                            r1.c_done(inst, "v")
                    C.bar("v", v, inst)
                    r1.c_done(v.nop(), "v")
                    if ps == 1 and sub == 1:
                        v.nop().then_inc(C.sem("l1f_attupd"), 1)
            if hp == 7:
                C.cnt["l1f_attupd"] += 1

    @C.block.tensor
    def _(t):
        t.wait_ge(C.sem("l1f_attupd"), C.cnt["l1f_attupd"])
    emit_t2fm_plain(C, "xi", "xiT", "l1f_xsT")

    ps1t = C.buf["ps1"]

    @C.block.tensor
    def _(t):
        t.wait_ge(C.sem("l1f_xsT"), C.cnt["l1f_xsT"])
        r1.free_count_wait(t)
        t.wait_ge(C.sem("l1f_nrm"), C.cnt["l1f_nrm"])
        for ch in range(4):
            for kt2 in range(KD):
                off = 1 if kt2 < 4 else 0
                inst = t.matmul(ps1t[0:1, ch * 512:(ch + 1) * 512], C.buf["ones_col"][:],
                                xiT[:, kt2 * ROW + off + ch * 512:
                                    kt2 * ROW + off + ch * 512 + 512],
                                start=(kt2 == 0), stop=(kt2 == KD - 1))
        inst.then_inc(C.sem("l1f_smA"), 1)
    C.cnt["l1f_smA"] += 1

    @C.block.scalar
    def _(s):
        s.wait_ge(C.sem("l1f_smA"), C.cnt["l1f_smA"])
        inst = s.copy(out=srow[0:1, :], in_=ps1t[0:1, 0:2048])
        C.bar("s", s, inst)
        s.nop().then_inc(C.sem("l1f_smAe"), 1)
    C.cnt["l1f_smAe"] += 1

    sqc_base = C.cnt["l1f_sqc"]
    sqcnt = 0
    for ch in range(4):
        for kt2 in range(KD):
            off = 1 if kt2 < 4 else 0
            qi = sqcnt % 4

            @C.block.scalar
            def _(s, ch=ch, kt2=kt2, off=off, qi=qi, sqcnt=sqcnt):
                if sqcnt == 0:
                    s.wait_ge(C.sem("l1f_smAe"), C.cnt["l1f_smAe"])
                if sqcnt >= 4:
                    s.wait_ge(C.sem("l1f_sqc"), sqc_base + sqcnt - 3)
                inst = s.activation(out=sqs[:, qi * 512:(qi + 1) * 512],
                                    in_=xiT[:, kt2 * ROW + off + ch * 512:
                                        kt2 * ROW + off + ch * 512 + 512],
                                    func=AF.Square)
                C.bar("s", s, inst)
                s.nop().then_inc(C.sem("l1f_sqk"), 1)
            C.cnt["l1f_sqk"] += 1

            @C.block.tensor
            def _(t, ch=ch, kt2=kt2, qi=qi):
                t.wait_ge(C.sem("l1f_sqk"), C.cnt["l1f_sqk"])
                t.matmul(ps1t[32:33, ch * 512:(ch + 1) * 512],
                         C.buf["ones_col"][:], sqs[:, qi * 512:(qi + 1) * 512],
                         start=(kt2 == 0), stop=(kt2 == KD - 1)
                         ).then_inc(C.sem("l1f_sqc"), 1)
            C.cnt["l1f_sqc"] += 1
            sqcnt += 1

    mub = C.buf["scr"][:, 0:2048]

    @C.block.scalar
    def _(s):
        s.wait_ge(C.sem("l1f_sqc"), C.cnt["l1f_sqc"])
        inst = s.copy(out=mub[0:1, :], in_=ps1t[32:33, 0:2048])
        C.bar("s", s, inst)
        s.nop().then_inc(C.sem("l1f_smB"), 1)
    C.cnt["l1f_smB"] += 1

    @C.block.vector
    def _(v):
        v.wait_ge(C.sem("l1f_smB"), C.cnt["l1f_smB"])
        inst = v.tensor_scalar(out=srow[0:1, :], in0=srow[0:1, :], scalar1=1.0 / D,
                               scalar2=None, op0=ALU.mult, op1=ALU.bypass)
        C.bar("v", v, inst)
        inst = v.tensor_tensor(out=rsqb[0:1, :], in0=srow[0:1, :], in1=srow[0:1, :],
                               op=ALU.mult)
        C.bar("v", v, inst)
        inst = v.scalar_tensor_tensor(out=mub[0:1, :], in0=mub[0:1, :],
                                      scalar=1.0 / D, in1=rsqb[0:1, :],
                                      op0=ALU.mult, op1=ALU.subtract)
        C.bar("v", v, inst)
        v.nop().then_inc(C.sem("l1f_var"), 1)
    C.cnt["l1f_var"] += 1

    @C.block.scalar
    def _(s):
        s.wait_ge(C.sem("l1f_var"), C.cnt["l1f_var"])
        inst = s.activation(out=mub[0:1, :], in_=mub[0:1, :], func=AF.Ln,
                            bias=C.constcol(EPS_LN)[0:1, :])
        C.bar("s", s, inst)
        inst = s.activation(out=mub[0:1, :], in_=mub[0:1, :], func=AF.Exp, scale=-0.5)
        C.bar("s", s, inst)
        s.nop().then_inc(C.sem("l1f_rstd"), 1)
    C.cnt["l1f_rstd"] += 1

    @C.block.tensor
    def _(t):
        t.wait_ge(C.sem("l1f_rstd"), C.cnt["l1f_rstd"])
        for cc in range(4):
            r0.p_wait(t)
            inst = t.matmul(r0.p_slot()[:, 0:512], C.buf["onesf_row"][0:1, :],
                            mub[0:1, cc * 512:(cc + 1) * 512], start=True, stop=True)
            r0.p_done(inst, "v")
        for cc in range(4):
            r0.p_wait(t)
            inst = t.matmul(r0.p_slot()[:, 0:512], C.buf["onesf_row"][0:1, :],
                            srow[0:1, cc * 512:(cc + 1) * 512], start=True, stop=True)
            r0.p_done(inst, "v")

    @C.block.vector
    def _(v):
        inst = None
        for cc in range(4):
            r0.c_wait(v)
            inst = v.tensor_copy(out=rsqb[:, cc * 512:(cc + 1) * 512],
                                 in_=r0.c_slot()[:, 0:512])
            r0.c_done(inst, "v")
        for cc in range(4):
            r0.c_wait(v)
            inst = v.tensor_copy(out=mub[:, cc * 512:(cc + 1) * 512],
                                 in_=r0.c_slot()[:, 0:512])
            if cc < 3:
                r0.c_done(inst, "v")
        C.bar("v", v, inst)
        r0.c_done(v.nop(), "v")
        v.nop().then_inc(C.sem("l1f_bcr"), 1)
    C.cnt["l1f_bcr"] += 1

    for kd in range(KD):
        off = 1 if kd < 4 else 0

        @C.block.vector
        def _(v, kd=kd, off=off):
            if kd == 0:
                v.wait_ge(C.sem("l1f_bcr"), C.cnt["l1f_bcr"])
            else:
                v.wait_ge(C.sem("l1f_nrm"), C.cnt["l1f_nrm"])
            inst = v.tensor_tensor(out=sqs[:], in0=xiT[:, kd * ROW + off: kd * ROW + off + 2048],
                                   in1=mub[:], op=ALU.subtract)
            C.bar("v", v, inst)
            inst = v.tensor_tensor(out=sqs[:], in0=sqs[:], in1=rsqb[:], op=ALU.mult)
            C.bar("v", v, inst)
            v.nop().then_inc(C.sem("l1f_nk"), 1)
        C.cnt["l1f_nk"] += 1

        @C.block.scalar
        def _(s, kd=kd):
            s.wait_ge(C.sem("l1f_nk"), C.cnt["l1f_nk"])
            if kd == 0:
                s.wait_ge(C.sem("l1f_smld"), smld)
            inst = s.activation(out=xiT[:, kd * ROW + 1: kd * ROW + 1 + 2048], in_=sqs[:],
                                func=AF.Identity, scale=gbc[:, kd:kd + 1],
                                bias=gbc[:, 8 + kd:9 + kd])
            C.bar("s", s, inst)
            s.nop().then_inc(C.sem("l1f_nrm"), 1)
        C.cnt["l1f_nrm"] += 1

    emit_l1f_ffn_mid(C, W, li)


def emit_l1f_ffn_mid(C, W, li):
    nc = C.nc
    xi, xiT = C.buf["xi"], C.buf["xiT"]
    prod, sqs = C.buf["prod"], C.buf["sqs"]
    big = C.buf["l1f_big"]
    w1s = C.buf["l1f_w1"]
    b1c, b2r = C.buf["l1f_b1"], C.buf["l1f_b2"]
    onesb = C.buf["ones_row"]
    r0, r1 = C.buf["rot0"], C.buf["rot1"]

    w1base = C.cnt["l1f_w1free"]
    w2base = C.cnt["l1f_w2free"]
    w1ld = (C.val("l1f_w1ldA"), C.val("l1f_w1ldB"))
    w2ld = C.val("l1f_w2ld")

    def w1_loads(g, us):
        for u in us:
            if w1base + u - 1 > 0:
                g.wait_ge(C.sem("l1f_w1free"), w1base + u - 1)
            j = u % 44
            nm = "l1f_w1ldA" if u % 2 == 0 else "l1f_w1ldB"
            g.dma_start(out=w1s[:, (u % 2) * 1024:(u % 2) * 1024 + 1024]
                        .rearrange("p (kt n) -> p kt n", n=128),
                        in_=W["l1aux_W1re"][li].rearrange("(kt p) n -> p kt n", p=128)
                        [:, :, 128 * j:128 * (j + 1)]).then_inc(C.sem(nm), 16)
            C.cnt[nm] += 16

    def w2_loads(g, us):
        for u in us:
            g.wait_ge(C.sem("l1f_w2free"), w2base + u + 1)
            nc2 = u % 2
            g.dma_start(out=big[:].rearrange("p (j n) -> p j n", n=512),
                        in_=W["l1aux_W2p"][li].rearrange("(j p) n -> p j n", p=128)
                        [:, :, nc2 * 512:(nc2 + 1) * 512]).then_inc(C.sem("l1f_w2ld"), 16)
            C.cnt["l1f_w2ld"] += 16

    @C.block.gpsimd
    def _(g):
        w1_loads(g, range(44))
        w2_loads(g, (0, 1))
        w1_loads(g, range(44, 88))
        w2_loads(g, (2, 3))
    C.cnt["l1f_w2free"] += 1

    @C.block.tensor
    def _(t):
        t.wait_ge(C.sem("l1f_attupd"), C.cnt["l1f_attupd"])
        t.nop().then_inc(C.sem("l1f_w2free"), 1)

    for ch in range(2):
        for j in range(44):
            u = ch * 44 + j

            @C.block.tensor
            def _(t, ch=ch, j=j, u=u):
                if j == 0 and ch == 0:
                    t.wait_ge(C.sem("l1f_nrm"), C.cnt["l1f_nrm"])
                t.wait_ge(C.sem("l1f_w1ldA" if u % 2 == 0 else "l1f_w1ldB"),
                          w1ld[u % 2] + 16 * (u // 2 + 1))
                r1.p_wait(t)
                slot = r1.p_slot()
                for nn2 in range(2):
                    for kt2 in range(KD):
                        inst = t.matmul(slot[:, nn2 * 512:(nn2 + 1) * 512],
                                        w1s[:, (u % 2) * 1024 + kt2 * 128:
                                            (u % 2) * 1024 + (kt2 + 1) * 128],
                                        xiT[:, kt2 * ROW + 1 + ch * 1024 + nn2 * 512:
                                            kt2 * ROW + 1 + ch * 1024 + nn2 * 512 + 512],
                                        start=(kt2 == 0), stop=(kt2 == KD - 1))
                r1.p_done(inst, "s" if j < 22 else "v")
                r1.self_bar_p(t)
                t.nop().then_inc(C.sem("l1f_w1free"), 1)
            C.cnt["l1f_w1free"] += 1

            if j < 22:
                @C.block.scalar
                def _(s, ch=ch, j=j):
                    if ch == 0 and j == 0:
                        s.wait_ge(C.sem("l1f_smld"), C.cnt["l1f_smld"])
                    r1.c_wait(s)
                    inst = s.activation(out=prod[:, j * 1024:(j + 1) * 1024], in_=r1.c_slot(),
                                        func=AF.Identity, bias=b1c[:, j:j + 1])
                    C.bar("s", s, inst)
                    r1.c_done(s.nop(), "s")
            else:
                jj = j - 22

                @C.block.vector
                def _(v, ch=ch, jj=jj):
                    r1.c_wait(v)
                    v.wait_ge(C.sem("l1f_pm"), C.cnt["l1f_pm"])
                    inst = v.tensor_copy(out=sqs[:, 0:1024], in_=r1.c_slot())
                    C.bar("v", v, inst)
                    r1.c_done(v.nop(), "v")
                    v.nop().then_inc(C.sem("l1f_gc"), 1)
                C.cnt["l1f_gc"] += 1

                @C.block.scalar
                def _(s, ch=ch, jj=jj):
                    s.wait_ge(C.sem("l1f_gc"), C.cnt["l1f_gc"])
                    inst = s.activation(out=sqs[:, 0:1024], in_=sqs[:, 0:1024], func=AF.Gelu,
                                        bias=b1c[:, 22 + jj:23 + jj])
                    C.bar("s", s, inst)
                    s.nop().then_inc(C.sem("l1f_gl"), 1)
                C.cnt["l1f_gl"] += 1

                @C.block.vector
                def _(v, ch=ch, jj=jj):
                    v.wait_ge(C.sem("l1f_gl"), C.cnt["l1f_gl"])
                    inst = v.tensor_tensor(out=prod[:, jj * 1024:(jj + 1) * 1024],
                                           in0=prod[:, jj * 1024:(jj + 1) * 1024],
                                           in1=sqs[:, 0:1024], op=ALU.mult)
                    inst.then_inc(C.sem("l1f_pm"), 1)
                C.cnt["l1f_pm"] += 1

        for nc2 in range(2):
            @C.block.tensor
            def _(t, ch=ch, nc2=nc2):
                t.wait_ge(C.sem("l1f_pm"), C.cnt["l1f_pm"])
                t.wait_ge(C.sem("l1f_w2ld"), w2ld + 16 * (2 * ch + nc2 + 1))
                if ch == 0 and nc2 == 0:
                    t.wait_ge(C.sem("l1f_smld"), C.cnt["l1f_smld"])
                for ts in range(8):
                    r0.p_wait(t)
                    slot = r0.p_slot()
                    for j in range(NJ):
                        t.matmul(slot, prod[:, j * 1024 + ts * 128: j * 1024 + ts * 128 + 128],
                                 big[:, j * 512:(j + 1) * 512],
                                 start=(j == 0), stop=False)
                    inst = t.matmul(slot, onesb[:, 0:128], b2r[:, nc2 * 512:(nc2 + 1) * 512],
                                    start=False, stop=True)
                    r0.p_done(inst, "v")
                r0.self_bar_p(t)
                t.nop().then_inc(C.sem("l1f_w2free"), 1)
                if ch == 1 and nc2 == 1:
                    t.nop().then_inc(C.sem("l1f_smfree"), 1)
            C.cnt["l1f_w2free"] += 1
            if ch == 1 and nc2 == 1:
                C.cnt["l1f_smfree"] += 1

            @C.block.vector
            def _(v, ch=ch, nc2=nc2):
                inst = None
                for ts in range(8):
                    tglob = ch * 8 + ts
                    r0.c_wait(v)
                    inst = v.tensor_tensor(
                        out=xi[:, tglob * D + nc2 * 512: tglob * D + nc2 * 512 + 512],
                        in0=xi[:, tglob * D + nc2 * 512: tglob * D + nc2 * 512 + 512],
                        in1=r0.c_slot(), op=ALU.add)
                    if ts < 7:
                        r0.c_done(inst, "v")
                C.bar("v", v, inst)
                r0.c_done(v.nop(), "v")
                if ch == 1 and nc2 == 1:
                    v.nop().then_inc(C.sem("l1f_upd"), 1)
    C.cnt["l1f_upd"] += 1



def emit_l0_ffn_weights(C, W):
    @C.block.gpsimd
    def _(g):
        g.dma_start(out=C.buf["u1b"][:].rearrange("p (k n) -> p k n", k=8),
                    in_=W["L0_ffn_U1"].rearrange("(k p) n -> p k n", p=128)).then_inc(C.sem("w_l0f"), 16)
        g.dma_start(out=C.buf["v1b"][:].rearrange("p (k n) -> p k n", k=2),
                    in_=W["L0_ffn_V1"].rearrange("(k p) n -> p k n", p=128)).then_inc(C.sem("w_l0f"), 16)
        g.dma_start(out=C.buf["u2b"][:].rearrange("p (k n) -> p k n", k=32),
                    in_=W["L0_ffn_U2"].rearrange("(k p) n -> p k n", p=128)).then_inc(C.sem("w_l0f"), 16)
        g.dma_start(out=C.buf["v2b"][:].rearrange("p (k n) -> p k n", k=2),
                    in_=W["L0_ffn_V2"].rearrange("(k p) n -> p k n", p=128)).then_inc(C.sem("w_l0f"), 16)
        g.dma_start(out=C.buf["b1b"][:], in_=W["L0_ffn_b1"][None, :]).then_inc(C.sem("w_l0f"), 16)
        g.dma_start(out=C.buf["b2b"][:], in_=W["L0_ffn_b2"][None, :]).then_inc(C.sem("w_l0f"), 16)
    C.cnt["w_l0f"] += 96


def emit_l0_ffn(C):
    nc = C.nc
    xnT = C.buf["xnT"]
    u1b, v1b, u2b, v2b = C.buf["u1b"], C.buf["v1b"], C.buf["u2b"], C.buf["v2b"]
    b1b, b2b = C.buf["b1b"], C.buf["b2b"]
    ones = C.buf["ones_row"]
    PT = C.buf["fmP"]
    QT = C.buf["fmQ"]
    Hc = C.buf["fmH"]
    r0 = C.buf["rot0"]
    r1 = C.buf["rot1"]
    res = C.buf["res"]
    wv = C.val("w_l0f")

    @C.block.tensor
    def _(t):
        t.wait_ge(C.sem("w_l0f"), wv)
        t.wait_ge(C.sem("xnT_done"), C.cnt["xnT_done"])
        for nt in range(2):
            for tc in range(4):
                r0.p_wait(t)
                slot = r0.p_slot()
                for kt in range(KD):
                    inst = t.matmul(slot, u1b[:, kt * 256 + nt * 128: kt * 256 + nt * 128 + 128],
                                    xnT[:, kt * S + tc * 512: kt * S + tc * 512 + 512],
                                    start=(kt == 0), stop=(kt == KD - 1))
                r0.p_done(inst, "s")

    @C.block.scalar
    def _(s):
        for nt in range(2):
            for tc in range(4):
                r0.c_wait(s)
                inst = s.copy(out=PT[:, nt * S + tc * 512: nt * S + tc * 512 + 512],
                              in_=r0.c_slot())
                r0.c_done(inst, "s")

    for tc in range(4):
        @C.block.tensor
        def _(t, tc=tc):
            for nt in range(32):
                r0.p_wait(t)
                slot = r0.p_slot()
                for kt in range(2):
                    t.matmul(slot, v1b[:, kt * 4096 + nt * 128: kt * 4096 + nt * 128 + 128],
                             PT[:, kt * S + tc * 512: kt * S + tc * 512 + 512],
                             start=(kt == 0), stop=False)
                inst = t.matmul(slot, b1b[:, nt * 128:(nt + 1) * 128], ones[:, 0:512],
                                start=False, stop=True)
                r0.p_done(inst, "s")

        @C.block.scalar
        def _(s, tc=tc):
            for nt in range(32):
                r0.c_wait(s)
                inst = s.activation(out=Hc[:, nt * 512:(nt + 1) * 512], in_=r0.c_slot(),
                                    func=AF.Gelu_apprx_tanh)
                r0.c_done(inst, "s")

        @C.block.tensor
        def _(t, tc=tc):
            for nt in range(2):
                r0.p_wait(t)
                slot = r0.p_slot()
                r0.free_count_wait(t)
                for kt in range(32):
                    inst = t.matmul(slot, u2b[:, kt * 256 + nt * 128: kt * 256 + nt * 128 + 128],
                                    Hc[:, kt * 512:(kt + 1) * 512],
                                    start=(kt == 0), stop=(kt == 31))
                r0.p_done(inst, "s")

        @C.block.scalar
        def _(s, tc=tc):
            for nt in range(2):
                r0.c_wait(s)
                inst = s.copy(out=QT[:, nt * S + tc * 512: nt * S + tc * 512 + 512],
                              in_=r0.c_slot())
                r0.c_done(inst, "s")

    @C.block.tensor
    def _(t):
        r0.free_count_wait(t)
        for tt in range(NT):
            r1.p_wait(t)
            slot = r1.p_slot()
            for nc2 in range(2):
                sl = slot[:, nc2 * 512:(nc2 + 1) * 512]
                for kt in range(2):
                    t.matmul(sl, QT[:, kt * S + tt * 128: kt * S + tt * 128 + 128],
                             v2b[:, kt * 1024 + nc2 * 512: kt * 1024 + nc2 * 512 + 512],
                             start=(kt == 0), stop=False)
                inst = t.matmul(sl, ones[:, 0:128], b2b[:, nc2 * 512:(nc2 + 1) * 512],
                                start=False, stop=True)
            r1.p_done(inst, "v")

    @C.block.vector
    def _(v):
        for tt in range(NT):
            r1.c_wait(v)
            slot = r1.c_slot()
            v.tensor_tensor(out=res[:, tt * D: tt * D + 512],
                            in0=res[:, tt * D: tt * D + 512], in1=slot[:, 0:512], op=ALU.add)
            inst = v.tensor_tensor(out=res[:, tt * D + 512: tt * D + 1024],
                                   in0=res[:, tt * D + 512: tt * D + 1024],
                                   in1=slot[:, 512:1024], op=ALU.add)
            if tt < NT - 1:
                r1.c_done(inst)
        C.bar("v", v, inst)
        r1.c_done(v.nop())
        v.nop().then_inc(C.sem("res_upd"), 1)
    C.cnt["res_upd"] += 1



def alloc_common(C, need_xnT=True):
    C.sbuf("res", [128, NT * D], F32)
    if need_xnT:
        C.sbuf("xnT", [128, KD * S], BF16)
        C.sbuf("bn6", [128, 16 * 24], F32)
        C.sbuf("mv16", [128, 32], F32)
        C.sbuf("nm16", [128, 16], F32)
        C.sbuf("rs16", [128, 16], F32)
    xns = C.sbuf("xnstage", [128, 2 * 2 * D], BF16)
    C.buf["rXN"] = Rot(C, "rXN", [xns[:, 0:2 * D], xns[:, 2 * D:4 * D]])
    psT = C.psum("psT", [128, 2048], BF16)
    C.buf["rotT"] = Rot(C, "rotT", [psT[:, 0:1024], psT[:, 1024:2048]])
    ps0 = C.psum("ps0", [128, 1024], F32)
    C.buf["rot0"] = Rot(C, "rot0", [ps0[:, 0:512], ps0[:, 512:1024]])
    ps1 = C.psum("ps1", [128, 2048], F32)
    C.buf["rot1"] = Rot(C, "rot1", [ps1[:, 0:1024], ps1[:, 1024:2048]])


def alloc_l0_ffn(C):
    C.sbuf("u1b", [128, 8 * 256], BF16)
    C.sbuf("v1b", [128, 2 * 4096], BF16)
    C.sbuf("u2b", [128, 32 * 256], BF16)
    C.sbuf("v2b", [128, 2 * 1024], BF16)
    C.sbuf("b1b", [1, 4096], BF16)
    C.sbuf("b2b", [1, 1024], BF16)
    C.sbuf("fmP", [128, 2 * S], BF16)
    C.sbuf("fmQ", [128, 2 * S], BF16)
    C.sbuf("fmH", [128, 32 * 512], BF16)


DEBUG_DUMP = []
L0A_PAIRS = 8
L0A_CONSTS = True
L0A_WOPASS = (3, 7)
L0A_SNAP = False
BATCH_BLOCKS = None
L0A_ATTN = True
L0A_WO = True
L1F_LAYERS = 6


def build_program(blocks, weight_names):
    nc = bass.Bass("TRN2", target_bir_lowering=False)
    x_in = nc.dram_tensor("x_in", [BPC, S, D], F32, kind="ExternalInput").ap()
    x_out = nc.dram_tensor("x_out", [BPC, S, D], F32, kind="ExternalOutput").ap()
    W = {n: nc.dram_tensor(n, list(sh), F32, kind="ExternalInput").ap()
         for (n, sh) in weight_names}

    with contextlib.ExitStack() as es:
        C = Ctx(nc, es)
        alloc_common(C, need_xnT=any(b in blocks for b in ("l0a", "l0f", "l1a")))
        if "l0a" in blocks:
            alloc_l0_attn(C)
            alloc_l0a_consts(C)
        if "l1a" in blocks:
            alloc_l1_attn(C)
        if "l0f" in blocks:
            alloc_l0_ffn(C)
        if "l1f" in blocks:
            alloc_l1_ffn(C)
        x_scr = None
        if "l1f" in blocks:
            x_scr = nc.dram_tensor("x_scr", [S, D], F32, kind="Internal").ap()
        sems = ["bar_v", "bar_s", "bar_t", "bar_g", "consts", "gpw",
                "res_load", "res_store", "res_upd",
                "ln_stats", "ln_rs", "ln_nmrs", "ln_g1", "ln_g2", "xnT_done"]
        if "l0a" in blocks or "l1f" in blocks:
            sems += ["l0c_g", "l0c_if", "l0c_v1", "l0c_s1", "l0c_v2", "l0c_s2",
                     "l0c_done"]
        if "l0a" in blocks:
            sems += ["l0_wld", "l0_wfree", "l0_gam", "l0_qkraw", "l0_qk", "l0_vfm",
                     "l0_vtm", "l0_g", "l0_rete", "l0_qkdone"]
        if "l0f" in blocks:
            sems += ["w_l0f"]
        if "l1a" in blocks:
            sems += ["l1ac_g", "l1ac_done", "l1a_wld", "l1a_wfree", "l1a_mpld",
                     "l1a_mpfree", "l1a_qk", "l1a_vp", "l1a_msk", "l1a_exp",
                     "l1a_y", "l1a_rc", "l1a_bc", "l1a_ogw", "l1a_att"]
        if "l1f" in blocks:
            sems += ["l1fc_g", "l1fc_s1", "l1fc_v1", "l1fc_s2", "l1fc_done",
                     "l1f_xif", "l1f_xi", "l1f_spl", "l1f_rld", "l1f_upd",
                     "l1f_smfree", "l1f_smld", "l1f_wfree",
                     "l1f_wldA", "l1f_wldB", "l1f_woldA", "l1f_woldB",
                     "l1f_xT", "l1f_xsT", "l1f_qkraw", "l1f_vf", "l1f_vT",
                     "l1f_vtm", "l1f_sq", "l1f_sums", "l1f_cl", "l1f_rsq",
                     "l1f_bc", "l1f_rope", "l1f_att", "l1f_mx", "l1f_ex",
                     "l1f_an", "l1f_aT", "l1f_aTe", "l1f_attupd", "l1f_smA",
                     "l1f_smAe", "l1f_sqk", "l1f_sqc", "l1f_smB", "l1f_var",
                     "l1f_rstd", "l1f_bcr", "l1f_nk", "l1f_nrm",
                     "l1f_w1free", "l1f_w1ldA", "l1f_w1ldB",
                     "l1f_w2free", "l1f_w2ld",
                     "l1f_pm", "l1f_gc", "l1f_gl"]
        for s in sems:
            C.sem(s)
        C.constcol(EPS_LN)
        C.constcol(math.log(0.125))
        if "l1f" in blocks:
            C.constcol(25.6 / 89.6)
        C.block = es.enter_context(nc.Block())

        emit_consts(C)
        if ("l0a" in blocks and L0A_CONSTS) or "l1f" in blocks:
            emit_l0a_consts(C)
        if "l1a" in blocks:
            emit_l1a_consts(C)
        if "l1f" in blocks:
            emit_l1f_consts(C)
        if "l0f" in blocks:
            emit_l0_ffn_weights(C, W)

        for b in range(BPC):
            bblocks = BATCH_BLOCKS[b] if BATCH_BLOCKS is not None else blocks
            emit_load_x(C, x_in, b)
            upd0 = C.cnt["res_upd"]
            if "l0a" in bblocks:
                emit_ln_t2fm(C)
                emit_l0_attn(C, W)
            if "l0f" in bblocks:
                emit_ln_t2fm(C)
                emit_l0_ffn(C)
            if "l1a" in bblocks:
                emit_ln_t2fm(C)
                emit_l1_attn(C, W)
            if "l1f" in bblocks:
                emit_l1_ffn(C, W, x_scr)
            if C.cnt["res_upd"] == upd0:
                _noop_res_upd(C)
            emit_store_x(C, x_out, b)

        if DEBUG_DUMP:
            dbg = {}
            for name in DEBUG_DUMP:
                t = C.buf[name]
                sh = [t.shape[0], t.shape[1]]
                dt = t.dtype
                dbg[name] = nc.dram_tensor("dbg_" + name, sh, dt, kind="ExternalOutput").ap()

            @C.block.sync
            def _(sy):
                sy.wait_ge(C.sem("res_store"), C.cnt["res_store"])
                for name in DEBUG_DUMP:
                    sy.dma_start(out=dbg[name][:, :], in_=C.buf[name][:]).then_inc(C.sem("res_store"), 16)
                    C.cnt["res_store"] += 16

        @C.block.sync
        def _(sy):
            sy.wait_ge(C.sem("res_store"), C.cnt["res_store"])

    _audit_sem_counts(nc, C)
    return nc


_SEM_UPD_RE = None


def _audit_sem_counts(nc, C):
    import re
    global _SEM_UPD_RE
    if _SEM_UPD_RE is None:
        _SEM_UPD_RE = re.compile(r"update:S\[(\w+)\]\+[+=](\d+)")
    totals = {}
    for inst in nc.all_instructions():
        if not inst.has_update():
            continue
        for m in _SEM_UPD_RE.finditer(inst.concise()):
            totals[m.group(1)] = totals.get(m.group(1), 0) + int(m.group(2))
    bad = []
    for name in C._sems:
        emitted = totals.get(name, 0)
        if emitted != C.cnt[name]:
            bad.append(f"sem '{name}': emitted {emitted} != counter {C.cnt[name]}")
    if bad:
        raise AssertionError("; ".join(bad))



WEIGHT_SHAPES = [
    ("L0_attn_wq", (D, D)), ("L0_attn_wk", (D, D)), ("L0_attn_wv", (D, D)),
    ("L0_attn_wg", (D, D)), ("L0_attn_wo", (D, D)),
    ("L0_ffn_U1", (D, 256)), ("L0_ffn_V1", (256, 4096)), ("L0_ffn_b1", (4096,)),
    ("L0_ffn_U2", (4096, 256)), ("L0_ffn_V2", (256, D)), ("L0_ffn_b2", (D,)),
    ("L1_attn_W_attn", (D, 2 * D)), ("L1_attn_v_tmp", (16, 16)), ("L1_attn_proj_tmp", (16, 16)),
    ("L1_ffn_Wqkv", (6, D, 3072)), ("L1_ffn_q_scale", (6, 64)), ("L1_ffn_k_scale", (6, 64)),
    ("L1_ffn_Wo", (6, D, D)), ("L1_ffn_ln_g", (6, D)), ("L1_ffn_ln_b", (6, D)),
    ("L1_ffn_W1", (6, D, 5460)), ("L1_ffn_b1", (6, 5460)),
    ("L1_ffn_W2", (6, 2730, D)), ("L1_ffn_b2", (6, D)),
]


AUX_BLOCK = {
    "l0a": [("l0aux_wqT", (D, D)), ("l0aux_wkT", (D, D)), ("l0aux_wvT", (D, D)),
            ("l0aux_wgT", (D, D)), ("l0aux_woT", (D, D))],
    "l0f": [],
    "l1a": [("l1aux_Mv", (D, D)), ("l1aux_Mproj", (D, D))],
    "l1f": [("l1aux_W1re", (6, D, 5632)), ("l1aux_sm", (6, 128, 62)),
            ("l1aux_W2p", (6, DIN, D))],
}


def prepare_aux(weights, blocks):
    aux = {}
    if "l0a" in blocks:
        for nm_, src_ in (("l0aux_wqT", "L0_attn_wq"), ("l0aux_wkT", "L0_attn_wk"),
                          ("l0aux_wvT", "L0_attn_wv"), ("l0aux_wgT", "L0_attn_wg"),
                          ("l0aux_woT", "L0_attn_wo")):
            aux[nm_] = np.ascontiguousarray(weights[src_].T)
    if "l1a" in blocks:
        ii, jj, dd = np.meshgrid(np.arange(16), np.arange(16), np.arange(64),
                                 indexing="ij")
        Mv = np.zeros((D, D), np.float32)
        Mv[ii * 64 + dd, jj * 64 + dd] = weights["L1_attn_v_tmp"][ii, jj]
        Mp = np.zeros((D, D), np.float32)
        Mp[jj * 64 + dd, ii * 64 + dd] = weights["L1_attn_proj_tmp"][ii, jj]
        aux["l1aux_Mv"] = Mv
        aux["l1aux_Mproj"] = Mp
    if "l1f" in blocks:
        W1 = weights["L1_ffn_W1"]
        W1re = np.zeros((6, D, 5632), np.float32)
        W1re[:, :, 0:2730] = W1[:, :, 0:2730]
        W1re[:, :, 2816:5546] = W1[:, :, 2730:5460]
        aux["l1aux_W1re"] = W1re
        b1 = weights["L1_ffn_b1"]
        b1re = np.zeros((6, 5632), np.float32)
        b1re[:, 0:2730] = b1[:, 0:2730]
        b1re[:, 2816:5546] = b1[:, 2730:5460]
        sm = np.zeros((6, 128, 62), np.float32)
        qs_, ks_ = weights["L1_ffn_q_scale"], weights["L1_ffn_k_scale"]
        sm[:, 0:64, 0] = qs_
        sm[:, 64:128, 0] = qs_
        sm[:, 0:64, 1] = ks_
        sm[:, 64:128, 1] = ks_
        sm[:, :, 2:10] = weights["L1_ffn_ln_g"].reshape(6, 8, 128).transpose(0, 2, 1)
        sm[:, :, 10:18] = weights["L1_ffn_ln_b"].reshape(6, 8, 128).transpose(0, 2, 1)
        sm[:, :, 18:62] = b1re.reshape(6, 44, 128).transpose(0, 2, 1)
        aux["l1aux_sm"] = np.ascontiguousarray(sm)
        W2p = np.zeros((6, DIN, D), np.float32)
        W2p[:, 0:2730, :] = weights["L1_ffn_W2"]
        aux["l1aux_W2p"] = W2p
    return aux


def _run_blocks(x, weights, blocks):
    used = [(n, sh) for (n, sh) in WEIGHT_SHAPES if _block_uses(n, blocks)]
    for b in blocks:
        used += AUX_BLOCK[b]
    aux = prepare_aux(weights, blocks)
    nc = build_program(blocks, used)
    in_maps = []
    for c in range(NCORES):
        m = {"x_in": np.ascontiguousarray(x[c * BPC:(c + 1) * BPC])}
        for (n, sh) in used:
            m[n] = aux[n] if n in aux else weights[n]
        in_maps.append(m)
    res = run_bass_kernel_spmd(nc, in_maps, core_ids=list(range(NCORES)))
    return np.concatenate([r["x_out"] for r in res.results], axis=0)


def _block_uses(wname, blocks):
    pre = {"l0a": "L0_attn", "l0f": "L0_ffn", "l1a": "L1_attn", "l1f": "L1_ffn"}
    return any(wname.startswith(pre[b]) for b in blocks)


def kernel(**inputs):
    x = np.asarray(inputs["L0_attn_x"], dtype=np.float32)
    weights = {n: np.ascontiguousarray(np.asarray(inputs[n], dtype=np.float32))
               for (n, _sh) in WEIGHT_SHAPES}
    for blocks in (("l0a",), ("l0f",), ("l1a",), ("l1f",)):
        x = _run_blocks(x, weights, blocks)
    return x
